# version 15
# baseline (speedup 1.0000x reference)
import numpy as np
import ml_dtypes
from contextlib import ExitStack

import concourse.bass as bass
import concourse.mybir as mybir
from concourse.bass import IndirectOffsetOnAxis
from concourse.bass_utils import run_bass_kernel_spmd

F32 = mybir.dt.float32
BF16 = mybir.dt.bfloat16
I32 = mybir.dt.int32
AF = mybir.ActivationFunctionType
ALU = mybir.AluOpType
AX = mybir.AxisListType

D = 2048
H = 16
FF = 1408
NE = 6
NKC = D // 128
NFFC = FF // 128
BLK = 512
NCORES = 8
EPS_LN = 1e-5


class _FakeInstr:
    def then_inc(self, *a, **k):
        return self


_FI = _FakeInstr()


class _Fake:
    def __getattr__(self, name):
        return lambda *a, **k: _FI


class Ctx:
    def __init__(self, engines, sems):
        self.q = engines["q"]
        self.s = engines["s"]
        self.v = engines["v"]
        self.t = engines["t"]
        self.g = engines["g"]
        self.sems = sems
        self.C = {k: 0 for k in sems}

    def inc(self, ins, key, n=1):
        self.C[key] += n
        ins.then_inc(self.sems[key], n)
        return self.C[key]

    def dq(self, sem, out, in_, **kw):
        return self.inc(self.q.dma_start(out=out, in_=in_, **kw), sem, 16)

    def dsc(self, sem, out, in_, **kw):
        return self.inc(self.s.dma_start(out=out, in_=in_, **kw), sem, 16)

    def dg(self, sem, out, in_, **kw):
        return self.inc(self.g.dma_start(out=out, in_=in_, **kw), sem, 16)

    def barrier(self):
        snap = dict(self.C)
        own = {"sA": self.s, "sV": self.v, "sT": self.t, "sP": self.g}
        for e in (self.q, self.s, self.v, self.t, self.g):
            for k, val in snap.items():
                if k.startswith("cc") or val == 0:
                    continue
                if own.get(k) is not e:
                    e.wait_ge(self.sems[k], val)


class Gd:

    def __init__(self):
        self.w = []

    def wait(self, eng, ctx):
        for k, v_ in self.w:
            eng.wait_ge(ctx.sems[k], v_)

    def set(self, *pairs):
        self.w = [p for p in pairs if p is not None]


def _mk_consts(nc):
    c = {}
    c["id_bf"] = nc.inline_tensor(np.eye(128, dtype=ml_dtypes.bfloat16), "id_bf").ap()
    c["id_f"] = nc.inline_tensor(np.eye(128, dtype=np.float32), "id_f").ap()
    tp1 = np.tile((np.arange(BLK, dtype=np.float32) + 1.0)[None, :], (128, 1))
    c["tp1"] = nc.inline_tensor(tp1, "tp1").ap()
    c["iota6"] = nc.inline_tensor(
        np.tile(np.arange(6, dtype=np.float32)[None, :], (128, 1)), "iota6").ap()
    iw1 = (np.arange(128, dtype=np.float32)[:, None]
           + 128.0 * np.arange(NKC, dtype=np.float32)[None, :])
    c["iotaW1"] = nc.inline_tensor(iw1, "iotaW1").ap()
    iw2 = (np.arange(128, dtype=np.float32)[:, None]
           + 128.0 * np.arange(NFFC, dtype=np.float32)[None, :])
    c["iotaW2"] = nc.inline_tensor(iw2, "iotaW2").ap()
    c["ones512_bf"] = nc.inline_tensor(np.ones((1, BLK), dtype=ml_dtypes.bfloat16), "ones512_bf").ap()
    c["onescol_bf"] = nc.inline_tensor(np.ones((128, 1), dtype=ml_dtypes.bfloat16), "onescol_bf").ap()
    c["onescol_f"] = nc.inline_tensor(np.ones((128, 1), dtype=np.float32), "onescol_f").ap()
    c["onesrow_f"] = nc.inline_tensor(np.ones((1, 128), dtype=np.float32), "onesrow_f").ap()
    c["onesrow_bf"] = nc.inline_tensor(np.ones((1, 128), dtype=ml_dtypes.bfloat16), "onesrow_bf").ap()
    return c


def build_nc(T):
    assert T % 1024 == 0
    SB = T // 2
    NTS = SB // 128
    NG = SB // BLK
    NB = T // BLK
    TOT_SEQ = 2 * T

    nc = bass.Bass("TRN2", target_bir_lowering=False)

    x_in = nc.dram_tensor("x", [T, D], F32, kind="ExternalInput").ap()
    wqkv = nc.dram_tensor("wqkv", [D, 3 * D], F32, kind="ExternalInput").ap()
    wout_d = nc.dram_tensor("wout", [D, D], F32, kind="ExternalInput").ap()
    decay = nc.dram_tensor("decay", [1, H], F32, kind="ExternalInput").ap()
    gateW = nc.dram_tensor("gateW", [NE, D], F32, kind="ExternalInput").ap()
    W1d = nc.dram_tensor("W1", [NE, D, FF], F32, kind="ExternalInput").ap()
    b1d = nc.dram_tensor("b1", [NE, FF], F32, kind="ExternalInput").ap()
    W2d = nc.dram_tensor("W2", [NE, FF, D], F32, kind="ExternalInput").ap()
    b2d = nc.dram_tensor("b2", [NE, D], F32, kind="ExternalInput").ap()
    pmask = nc.dram_tensor("pmask", [128, NCORES], F32, kind="ExternalInput").ap()
    gpmask = nc.dram_tensor("gpmask", [NCORES, 1], F32, kind="ExternalInput").ap()
    out_d = nc.dram_tensor("out", [T, D], F32, kind="ExternalOutput").ap()

    wqkv_bf = nc.dram_tensor("wqkv_bf", [D, 3 * D], BF16).ap()
    wout_bf = nc.dram_tensor("wout_bf", [D, D], BF16).ap()
    w1s_bf = nc.dram_tensor("w1s_bf", [D, FF], BF16).ap()
    w2s_bf = nc.dram_tensor("w2s_bf", [2, FF, D], BF16).ap()
    q_sp = nc.dram_tensor("q_sp", [D, T], BF16).ap()
    skv_sp = nc.dram_tensor("skv_sp", [D, T], F32).ap()
    sk_sp = nc.dram_tensor("sk_sp", [D, T], F32).ap()
    x1_sp = nc.dram_tensor("x1_sp", [T, D], F32).ap()
    ln2T_sp = nc.dram_tensor("ln2T_sp", [D, T], BF16).ap()
    h_sp = nc.dram_tensor("h_sp", [2, FF, T], BF16).ap()
    ag_in = nc.dram_tensor("ag_in", [128, 2 * H], F32).ap()
    ag_out = nc.dram_tensor("ag_out", [NCORES * 128, 2 * H], F32, addr_space="Shared").ap()
    g_in = nc.dram_tensor("g_in", [1, D], F32).ap()
    g_out = nc.dram_tensor("g_out", [NCORES, D], F32, addr_space="Shared").ap()

    cst = _mk_consts(nc)
    constv_d = nc.inline_tensor(
        np.tile(np.array([[EPS_LN, -0.5, -1.0]], dtype=np.float32), (128, 1)),
        "constv_dram").ap()

    es = ExitStack()
    sb_ = lambda n, s, d: es.enter_context(nc.sbuf_tensor(n, s, d))
    ps_ = lambda n, s, d: es.enter_context(nc.psum_tensor(n, s, d))
    sem_ = lambda n: es.enter_context(nc.semaphore(n))

    BIGBF = max(NKC * SB, NKC * NKC * 128, NKC * FF, 2 * NFFC * 1024)
    SMBF = max(3 * NKC * 256 + 2 * D + 2 * SB,
               NKC * 512 + 2048 + 2048 + 2 * D + 2 * NKC * 128,
               NKC * 512 + NFFC * 512,
               2 * 2 * NFFC * 128)
    F32N = max(3 * D + 5 * SB + 4096,
               5 * D + 2 * 2048 + 2048 + 2048 + 2048 + 1024,
               12288 + D, 19968 + 2 * 640)
    bigbf = sb_("bigbf", [128, BIGBF], BF16)
    smbf = sb_("smbf", [128, SMBF], BF16)
    fpool = sb_("fpool", [128, F32N], F32)

    st_stats = sb_("st_stats", [128, 16], F32)
    st_partials = sb_("st_partials", [128, 8], F32)
    d_row = sb_("d_row", [1, H], F32)
    lnd_row = sb_("lnd_row", [1, H], F32)
    d_col = sb_("d_col", [128, H], F32)
    lnd_col = sb_("lnd_col", [128, H], F32)
    lndb_col = sb_("lndb_col", [128, H], F32)
    carry = sb_("carry", [128, 2 * H], F32)
    carryP = sb_("carryP", [128, 2 * H], F32)
    carryT = sb_("carryT", [128, 2 * H], F32)
    pmask_t = sb_("pmask_t", [128, NCORES], F32)
    gpm_t = sb_("gpm_t", [NCORES, 1], F32)
    idbf_t = sb_("idbf_t", [128, 128], BF16)
    idf_t = sb_("idf_t", [128, 128], F32)
    tp1_t = sb_("tp1_t", [128, BLK], F32)
    iota6_t = sb_("iota6_t", [128, NE], F32)
    iotaW1_t = sb_("iotaW1_t", [128, NKC], F32)
    iotaW2_t = sb_("iotaW2_t", [128, NFFC], F32)
    ones512bf_t = sb_("ones512bf_t", [1, BLK], BF16)
    onescolbf_t = sb_("onescolbf_t", [128, 1], BF16)
    onescolf_t = sb_("onescolf_t", [128, 1], F32)
    onesrowf_t = sb_("onesrowf_t", [1, 128], F32)
    onesrowbf_t = sb_("onesrowbf_t", [1, 128], BF16)
    gacc = sb_("gacc", [128, D], F32)
    gWT = sb_("gWT", [128, NKC * NE], F32)
    meanT = sb_("meanT", [128, NKC], F32)
    lgrow = sb_("lgrow", [1, NE], F32)
    gG = sb_("gG", [128, NE], F32)
    gE = sb_("gE", [128, NE], F32)
    gP = sb_("gP", [128, NE], F32)
    gm1 = sb_("gm1", [128, 1], F32)
    gm2 = sb_("gm2", [128, 1], F32)
    gneg = sb_("gneg", [128, 1], F32)
    gsum = sb_("gsum", [128, 1], F32)
    grv = sb_("grv", [128, 1], F32)
    gmask1 = sb_("gmask1", [128, NE], F32)
    gmask2 = sb_("gmask2", [128, NE], F32)
    gtmp = sb_("gtmp", [128, NE], F32)
    gp2 = sb_("gp2", [128, NE], F32)
    gvs = sb_("gvs", [128, 1], F32)
    gw1 = sb_("gw1", [128, 1], F32)
    gw2 = sb_("gw2", [128, 1], F32)
    geidx = [sb_("geidx%d" % i, [128, 1], F32) for i in range(2)]
    gesc = [sb_("gesc%d" % i, [128, 1], F32) for i in range(2)]
    idxf = sb_("idxf", [128, NKC], F32)
    idxi1 = [sb_("idxi1_%d" % i, [128, NKC], I32) for i in range(2)]
    idxf2 = sb_("idxf2", [128, NFFC], F32)
    idxi2 = [sb_("idxi2_%d" % i, [128, NFFC], I32) for i in range(2)]
    idxbi = sb_("idxbi", [128, 2], I32)
    qe_a = sb_("qe_a", [128, BLK], F32)
    qe_b = sb_("qe_b", [128, BLK], F32)

    constv = sb_("constv", [128, 4], F32)
    nc.const_aps.register(F32, EPS_LN, constv[:, 0:1])
    nc.const_aps.register(F32, -0.5, constv[:, 1:2])
    nc.const_aps.register(F32, -1.0, constv[:, 2:3])

    psA = [ps_("psA%d" % i, [128, 512], F32) for i in range(4)]
    psT = [ps_("psT%d" % i, [128, 1024], BF16) for i in range(2)]
    psD = [ps_("psD%d" % i, [128, 512], F32) for i in range(2)]

    SEMKEYS = (["sV", "sA", "sT", "sP", "cc1", "cc2",
                "cst", "xa0", "xa1", "wq0", "wq1", "wq2", "wq3", "qsp0", "qsp1", "skvsp", "sksp",
                "agi", "ago", "wout", "xc0", "xc1", "xc2", "xc3",
                "rq0", "rq1", "rkv0", "rkv1", "rk0", "rk1",
                "x1sp0", "x1sp1", "x1sp2", "x1sp3", "l2sp0", "l2sp1",
                "gin", "gout", "bg", "w1g0", "w1g1", "rvh0", "rvh1", "rkh0", "rkh1", "w2g0", "w2g1", "wcast", "wcast2", "wstg", "wld", "bgg0", "bgg1", "bgg2", "bgg3", "l2r0", "l2r1", "hsp0", "hsp1",
                "x1r0", "x1r1", "hr0", "hr1", "out0", "out1"])
    sems = {k: sem_(k) for k in SEMKEYS}

    A_XT = 0
    A_SCR = 2 * D
    A_PHIK = 3 * D
    A_VH = 3 * D + SB
    A_KVIN = 3 * D + 2 * SB
    A_SKV = 3 * D + 3 * SB
    A_SK = 3 * D + 4 * SB
    C_X1 = 0
    C_SCR = 4 * D
    C_SKVB = 5 * D
    C_SKB = 5 * D + 2048
    C_ST = 5 * D + 2 * 2048
    C_STK = C_ST + 2048
    C_T2 = C_STK + 2048
    C_WSB = C_T2
    C_DPW = C_T2 + 2048
    E_X1 = 0
    E_OUT = 2 * D

    A_WQ = 0
    A_LNT = 3 * NKC * 256
    A_QSTG = A_LNT + 2 * D
    C_OUTPRE = 0
    C_QH = NKC * 512
    C_QS = C_QH + 2048
    C_LN2 = C_QS + 2048
    C_LN2T = C_LN2 + 2 * D
    E_LN2T = 0
    E_HST = 2 * NKC * 512
    E_HBF = 0

    def fp(off, n):
        return fpool[:, off:off + n]

    wgather = fpool[:, 12288:12288 + D]
    grow = fpool[0:1, C_SKVB:C_SKVB + D]
    rminbf = fpool[0:1, C_SCR:C_SCR + 2 * BLK].bitcast(BF16).rearrange(
        "p (s t) -> p s t", s=2)
    gWsb = fpool[0:NE, C_SKB:C_SKB + D]
    agbuf = fpool[:, 0:NCORES * 2 * H].rearrange("p (r c) -> p r c", r=NCORES)
    wgather2 = [fpool[:, 12288:12288 + 1024], fpool[:, 12288 + 1024:12288 + D]]
    gout_sb = gacc[0:NCORES, :]
    lnrows = fpool[0:1, C_SCR:C_SCR + 2 * BLK].rearrange("p (s t) -> p s t", s=2)
    denrows = fpool[0:1, C_SCR + 2 * BLK:C_SCR + 4 * BLK].rearrange("p (s t) -> p s t", s=2)
    b1row = fpool[0:1, 6144:6144 + FF].bitcast(BF16).rearrange("p (s f) -> p s f", s=2)
    b2crow = fpool[0:1, 7552:7552 + 1024].bitcast(BF16)

    def sp(off, n):
        return smbf[:, off:off + n]

    def walk(ctx: Ctx):
        q, s, v, t, g = ctx.q, ctx.s, ctx.v, ctx.t, ctx.g
        S = ctx.sems
        inc = ctx.inc
        dq, dsc, dg = ctx.dq, ctx.dsc, ctx.dg

        for dst, src in [
            (pmask_t[:, :], pmask), (gpm_t[:, :], gpmask),
            (idbf_t[:, :], cst["id_bf"]), (idf_t[:, :], cst["id_f"]),
            (tp1_t[:, :], cst["tp1"]), (iota6_t[:, :], cst["iota6"]),
            (iotaW1_t[:, :], cst["iotaW1"]), (iotaW2_t[:, :], cst["iotaW2"]),
            (ones512bf_t[:, :], cst["ones512_bf"]),
            (onescolbf_t[:, :], cst["onescol_bf"]),
            (onescolf_t[:, :], cst["onescol_f"]),
            (onesrowf_t[:, :], cst["onesrow_f"]),
            (onesrowbf_t[:, :], cst["onesrow_bf"]),
            (d_row[:, :], decay), (gWsb[:, :], gateW),
            (constv[:, 0:3], constv_d),
        ]:
            ev_const = dq("cst", dst, src[:, :])
        s.wait_ge(S["cst"], ev_const)
        ev = inc(s.activation(d_row[:, :], d_row[:, :], AF.Sigmoid), "sA")
        s.wait_ge(S["sA"], ev)
        ev_lnd = inc(s.activation(lnd_row[:, :], d_row[:, :], AF.Ln), "sA")
        t.wait_ge(S["sA"], ev_lnd)
        inc(t.matmul(psD[0][:, 0:H], onesrowf_t[:, :], d_row[:, :],
                     start=True, stop=True), "sT")
        ev_t = inc(t.matmul(psD[0][:, H:2 * H], onesrowf_t[:, :], lnd_row[:, :],
                            start=True, stop=True), "sT")
        v.wait_ge(S["sT"], ev_t)
        inc(v.tensor_copy(d_col[:, :], psD[0][:, 0:H]), "sV")
        ev_dcol = inc(v.tensor_copy(lnd_col[:, :], psD[0][:, H:2 * H]), "sV")
        ev_gz = inc(g.memset(gacc[:, :], 0.0), "sP")
        for rc in range(NKC):
            dg("wcast", wqkv_bf[rc * 128:(rc + 1) * 128, :],
               wqkv[rc * 128:(rc + 1) * 128, :])
        ev_wqcast = ctx.C["wcast"]
        for rc in range(NKC):
            dg("wcast2", wout_bf[rc * 128:(rc + 1) * 128, :],
               wout_d[rc * 128:(rc + 1) * 128, :])
        ev_wocast = ctx.C["wcast2"]

        gd_psD = [Gd(), Gd()]
        gd_psD[0].set(("sV", ev_dcol))
        gd_psD[1].set(("sV", ev_dcol))

        gd_xt = [Gd(), Gd()]
        gd_lnt = [Gd(), Gd()]
        gd_psT = [Gd(), Gd()]
        gd_psA = [Gd() for _ in range(6)]
        gd_wq = [Gd(), Gd(), Gd()]
        gd_qstg = [Gd(), Gd()]
        gd_skv = Gd()
        gd_sk = Gd()
        gd_scrA = Gd()
        gd_head = Gd()
        gd_qe = Gd()
        gd_qeg = [Gd() for _ in range(4)]
        carry_ev = {}
        stats_g = [0, 0]
        lnxT = bigbf[:, 0:NKC * SB].rearrange("p (c t) -> p c t", c=NKC)
        wq3 = [sp(A_WQ + i * NKC * 256, NKC * 256)
               .rearrange("p (kc s m) -> p kc s m", kc=NKC, s=2) for i in range(3)]
        lnxT_rd_ev = None

        for sb in range(2):
            ev_lnx_all = 0
            for tt in range(NTS):
                slot = tt % 2
                xt = fp(A_XT + slot * D, D)
                gd_xt[slot].wait(q, ctx)
                ev_x = dq("xa%d" % slot, xt,
                          x_in[sb * SB + tt * 128: sb * SB + (tt + 1) * 128, :])
                st0 = slot * 8
                sumsq = st_stats[:, st0:st0 + 1]
                sumr = st_stats[:, st0 + 1:st0 + 2]
                mean = st_stats[:, st0 + 2:st0 + 3]
                ex2 = st_stats[:, st0 + 3:st0 + 4]
                msq = st_stats[:, st0 + 4:st0 + 5]
                var = st_stats[:, st0 + 5:st0 + 6]
                rstd = st_stats[:, st0 + 6:st0 + 7]
                negmr = st_stats[:, st0 + 7:st0 + 8]
                scr = fp(A_SCR, D)
                v.wait_ge(S["xa%d" % slot], ev_x)
                if stats_g[slot]:
                    v.wait_ge(S["sV"], stats_g[slot])
                ev1 = inc(v.reduce_sum(sumr, xt, axis=AX.X), "sV")
                s.wait_ge(S["xa%d" % slot], ev_x)
                if stats_g[slot]:
                    s.wait_ge(S["sV"], stats_g[slot])
                gd_scrA.wait(s, ctx)
                ev2 = inc(s.activation(scr, xt, AF.Square, accum_out=sumsq), "sA")
                gd_scrA.set(("sA", ev2))
                v.wait_ge(S["sA"], ev2)
                v.wait_ge(S["sV"], ev1)
                inc(v.tensor_scalar_mul(mean, sumr, 1.0 / D), "sV")
                ev3 = inc(v.tensor_scalar_mul(ex2, sumsq, 1.0 / D), "sV")
                v.wait_ge(S["sV"], ev3)
                ev4 = inc(v.tensor_scalar(msq, mean, mean, None, ALU.mult), "sV")
                v.wait_ge(S["sV"], ev4)
                ev5 = inc(v.tensor_sub(var, ex2, msq), "sV")
                s.wait_ge(S["sV"], ev5)
                ev6 = inc(s.activation(msq, var, AF.Ln, bias=EPS_LN), "sA")
                s.wait_ge(S["sA"], ev6)
                ev7 = inc(s.activation(rstd, msq, AF.Exp, scale=-0.5), "sA")
                lnt = sp(A_LNT + slot * D, D)
                gd_lnt[slot].wait(v, ctx)
                v.wait_ge(S["sA"], ev7)
                ev9 = inc(v.tensor_scalar(lnt, xt, mean, rstd,
                                          ALU.subtract, ALU.mult), "sV")
                stats_g[slot] = ev9
                gd_xt[slot].set(("sV", ev9), ("sA", ev2))
                evt = None
                for half in range(2):
                    gd_psT[half].wait(t, ctx)
                    t.wait_ge(S["sV"], ev9)
                    if lnxT_rd_ev is not None:
                        t.wait_ge(S["sT"], lnxT_rd_ev)
                    for cc in range(8):
                        c = half * 8 + cc
                        ins = t.transpose(psT[half][:, cc * 128:(cc + 1) * 128],
                                          lnt[:, c * 128:(c + 1) * 128], idbf_t[:, :])
                    evt = inc(ins, "sT")
                    v.wait_ge(S["sT"], evt)
                    ins = v.tensor_copy(
                        lnxT[:, half * 8:(half + 1) * 8, tt * 128:(tt + 1) * 128],
                        psT[half][:, :].rearrange("p (c m) -> p c m", c=8))
                    evv = inc(ins, "sV")
                    gd_psT[half].set(("sV", evv))
                    ev_lnx_all = max(ev_lnx_all, evv)
                gd_lnt[slot].set(("sT", evt))

            wqv = wqkv_bf.rearrange("(kc p) (g h m) -> p kc g h m", p=128, g=3, m=128)
            pairs = [("kv", h) for h in range(H)] + [("q", j) for j in range(8)]
            banks6 = psA + psD
            rot = 0
            for pi, (kind, j) in enumerate(pairs):
                wslot = pi % 3
                gd_wq[wslot].wait(s, ctx)
                s.wait_ge(S["wcast"], ev_wqcast)
                wqflat = sp(A_WQ + wslot * NKC * 256, NKC * 256).rearrange(
                    "p (kc m) -> p kc m", kc=NKC)
                if kind == "q":
                    dsc("wq%d" % wslot, wqflat[:, :, :],
                        wqkv_bf[:, 2 * j * 128:(2 * j + 2) * 128]
                        .rearrange("(kc p) m -> p kc m", p=128))
                else:
                    dsc("wq%d" % wslot, wq3[wslot][:, :, 0, :], wqv[:, :, 1, j, :])
                    dsc("wq%d" % wslot, wq3[wslot][:, :, 1, :], wqv[:, :, 2, j, :])
                ev_w = ctx.C["wq%d" % wslot]
                ev_mm_last = None
                for s01 in range(2):
                    if kind == "q":
                        oc = 2 * j + s01
                        is_q, is_k = True, False
                    else:
                        oc = (16 + j) if s01 == 0 else (32 + j)
                        is_q, is_k = False, (s01 == 0)
                    hv = j
                    mybanks = [banks6[(rot + i) % 6] for i in range(NG)]
                    mybank_ix = [(rot + i) % 6 for i in range(NG)]
                    rot = (rot + NG) % 6
                    for i in range(NG):
                        gd_psA[mybank_ix[i]].wait(t, ctx)
                    t.wait_ge(S["wq%d" % wslot], ev_w)
                    t.wait_ge(S["sV"], ev_lnx_all)
                    for kc in range(NKC):
                        if kind == "q":
                            lhs = wqflat[:, kc, s01 * 128:(s01 + 1) * 128]
                        else:
                            lhs = wq3[wslot][:, kc, s01, :]
                        for grp in range(NG):
                            ins = t.matmul(mybanks[grp][:, :], lhs,
                                           lnxT[:, kc, grp * 512:(grp + 1) * 512],
                                           start=(kc == 0), stop=(kc == NKC - 1))
                    ev_mm = inc(ins, "sT")
                    ev_mm_last = ev_mm
                    ev_evac_last = None
                    for grp in range(NG):
                        bank = mybanks[grp]
                        bix = mybank_ix[grp]
                        if is_q or is_k:
                            qeA = fp(16384 + grp * 512, 512)
                            qeB = fp(18432 + grp * 512, 512)
                            gd_qeg[grp].wait(s, ctx)
                            s.wait_ge(S["sT"], ev_mm)
                            eva = inc(s.activation(qeA, bank[:, :], AF.Exp), "sA")
                            v.wait_ge(S["sA"], eva)
                            evb = inc(v.tensor_scalar_min(qeB, qeA, 1.0), "sV")
                            if is_q:
                                qsl = (2 * j + s01) % 2
                                dst = sp(A_QSTG + qsl * SB + grp * 512, 512)
                                if grp == 0:
                                    gd_qstg[qsl].wait(v, ctx)
                            else:
                                dst = fp(A_PHIK + grp * 512, 512)
                                if grp == 0:
                                    gd_head.wait(v, ctx)
                            v.wait_ge(S["sV"], evb)
                            v.wait_ge(S["sT"], ev_mm)
                            ev_evac = inc(v.scalar_tensor_tensor(
                                out=dst, in0=bank[:, :], scalar=0.0,
                                in1=qeB, op0=ALU.max, op1=ALU.add), "sV")
                            gd_psA[bix].set(("sV", ev_evac))
                            gd_qeg[grp].set(("sV", ev_evac))
                        else:
                            s.wait_ge(S["sT"], ev_mm)
                            if grp == 0:
                                gd_head.wait(s, ctx)
                            ev_evac = inc(s.activation(fp(A_VH + grp * 512, 512),
                                                       bank[:, :], AF.Copy), "sA")
                            gd_psA[bix].set(("sA", ev_evac))
                        ev_evac_last = ev_evac
                    if is_k:
                        k_done_ev = ev_evac_last
                    if is_q:
                        qsl = oc % 2
                        q.wait_ge(S["sV"], ev_evac_last)
                        ev_sp = dq("qsp%d" % qsl,
                                   q_sp[oc * 128:(oc + 1) * 128, sb * SB:(sb + 1) * SB],
                                   sp(A_QSTG + qsl * SB, SB))
                        gd_qstg[qsl].set(("qsp%d" % qsl, ev_sp))
                    elif not is_k:
                        phik = fp(A_PHIK, SB)
                        vh = fp(A_VH, SB)
                        kvin = fp(A_KVIN, SB)
                        dfull = d_col[:, hv:hv + 1].broadcast_to([128, SB])
                        skv = fp(A_SKV, SB)
                        skb = fp(A_SK, SB)
                        v.wait_ge(S["sA"], ev_evac_last)
                        v.wait_ge(S["sV"], k_done_ev)
                        gd_head.wait(v, ctx)
                        ev_kv = inc(v.tensor_mul(kvin, phik, vh), "sV")
                        v.wait_ge(S["sV"], ev_dcol)
                        gd_skv.wait(v, ctx)
                        init_kv = carry[:, hv:hv + 1] if sb > 0 else 0.0
                        init_k = carry[:, H + hv:H + hv + 1] if sb > 0 else 0.0
                        if sb > 0:
                            v.wait_ge(S["sV"], carry_ev[("kv", hv)])
                            v.wait_ge(S["sV"], carry_ev[("k", hv)])
                        v.wait_ge(S["sV"], ev_kv)
                        ev_skv = inc(v.tensor_tensor_scan(skv, dfull, kvin, init_kv,
                                                          ALU.mult, ALU.add), "sV")
                        gd_sk.wait(v, ctx)
                        ev_sk = inc(v.tensor_tensor_scan(skb, dfull, phik, init_k,
                                                         ALU.mult, ALU.add), "sV")
                        v.wait_ge(S["sV"], ev_skv)
                        ev_ckv = inc(v.tensor_copy(carry[:, hv:hv + 1],
                                                   skv[:, SB - 1:SB]), "sV")
                        v.wait_ge(S["sV"], ev_sk)
                        ev_ck = inc(v.tensor_copy(carry[:, H + hv:H + hv + 1],
                                                  skb[:, SB - 1:SB]), "sV")
                        carry_ev[("kv", hv)] = ev_ckv
                        carry_ev[("k", hv)] = ev_ck
                        q.wait_ge(S["sV"], ev_ckv)
                        e1_ = dq("skvsp", skv_sp[hv * 128:(hv + 1) * 128,
                                                 sb * SB:(sb + 1) * SB], skv)
                        q.wait_ge(S["sV"], ev_ck)
                        e2_ = dq("sksp", sk_sp[hv * 128:(hv + 1) * 128,
                                               sb * SB:(sb + 1) * SB], skb)
                        gd_skv.set(("skvsp", e1_))
                        gd_sk.set(("sksp", e2_))
                        gd_head.set(("sV", ev_sk), ("sV", ev_skv))
                gd_wq[wslot].set(("sT", ev_mm_last))
            lnxT_rd_ev = ctx.C["sT"]

        ev_c_all = max(carry_ev.values())
        g.wait_ge(S["sV"], ev_c_all)
        ev_agin = dg("agi", ag_in[:, :], carry[:, :])
        g.wait_ge(S["agi"], ev_agin)
        ev_cc1 = inc(g.collective_compute(
            "AllGather", ALU.bypass, replica_groups=[list(range(NCORES))],
            ins=[ag_in.opt()], outs=[ag_out.opt()]), "cc1")
        q.wait_ge(S["cc1"], ev_cc1)
        ev_ag = dq("ago", agbuf[:, :, :], ag_out.rearrange("(r p) c -> p r c", p=128))
        v.wait_ge(S["ago"], ev_ag)
        ev_cp = inc(v.tensor_scalar_mul(carryT[:, :], agbuf[:, 0, :],
                                        pmask_t[:, 0:1]), "sV")
        for r in range(1, NCORES):
            src, dst = (carryT, carryP) if r % 2 == 1 else (carryP, carryT)
            v.wait_ge(S["sV"], ev_cp)
            ev_cp = inc(v.scalar_tensor_tensor(
                out=dst[:, :], in0=agbuf[:, r, :], scalar=pmask_t[:, r:r + 1],
                in1=src[:, :], op0=ALU.mult, op1=ALU.add), "sV")
        ev_carryP = ev_cp

        woutb = bigbf[:, 0:NKC * NKC * 128].rearrange(
            "p (kc oc m) -> p kc oc m", kc=NKC, oc=NKC)
        woutb2 = bigbf[:, 0:NKC * NKC * 128].rearrange(
            "p (kc ocm) -> p kc ocm", kc=NKC)
        q.wait_ge(S["sT"], lnxT_rd_ev)
        q.wait_ge(S["wcast2"], ev_wocast)
        for kc in range(NKC):
            ev_wout = dq("wout", woutb[:, kc, :, :],
                         wout_bf[kc * 128:(kc + 1) * 128, :]
                         .rearrange("p (oc m) -> p oc m", oc=NKC))
        ctx.barrier()
        outpre = sp(C_OUTPRE, NKC * 512).rearrange("p (c t) -> p c t", c=NKC)
        gd_x1 = [Gd() for _ in range(4)]
        gd_qh = [Gd(), Gd()]
        gd_psAc = [Gd() for _ in range(4)]
        gd_dpow = [Gd() for _ in range(4)]
        gd_st = [Gd() for _ in range(4)]
        gd_stk = [Gd() for _ in range(4)]
        gd_t2 = [Gd() for _ in range(4)]
        gd_qs = [Gd() for _ in range(4)]
        gd_wsb = [Gd(), Gd()]
        gd_rows = [Gd(), Gd()]
        gd_outpre = Gd()
        prev_ga = 0
        gacc_pend = []
        gacc_srcs_pend = []
        gacc_srcs = []
        ln2_pool_g = [0, 0]
        prev_bc = [0, 0]
        gd_ln2 = [Gd(), Gd()]
        gd_ln2T = [Gd(), Gd()]
        gd_scrC = Gd()
        for b_ in range(4):
            gd_psAc[b_].set(("sA", ctx.C["sA"]), ("sV", ctx.C["sV"]))
        gd_psD[0].set(("sA", ctx.C["sA"]), ("sV", ctx.C["sV"]))
        gd_psD[1].set(("sA", ctx.C["sA"]), ("sV", ctx.C["sV"]))

        relev = {}

        def issue_reload(hp, blk_):
            rb = hp % 2
            gd_qh[rb].wait(s, ctx)
            gd_qh[rb].wait(q, ctx)
            qb = sp(C_QH + rb * 1024, 1024).rearrange("p (hh t) -> p hh t", hh=2)
            skvb = fp(C_SKVB + rb * 1024, 1024).rearrange("p (hh t) -> p hh t", hh=2)
            skb2 = fp(C_SKB + rb * 1024, 1024).rearrange("p (hh t) -> p hh t", hh=2)
            e1_ = dsc("rq%d" % rb, qb[:, :, :],
                      q_sp[hp * 256:(hp + 1) * 256, blk_ * BLK:(blk_ + 1) * BLK]
                      .rearrange("(hh p) t -> p hh t", p=128))
            e2a = dq("rkv%d" % rb, skvb[:, 0, :],
                     skv_sp[hp * 256:hp * 256 + 128, blk_ * BLK:(blk_ + 1) * BLK])
            e2b = dsc("rvh%d" % rb, skvb[:, 1, :],
                      skv_sp[hp * 256 + 128:hp * 256 + 256,
                             blk_ * BLK:(blk_ + 1) * BLK])
            e3a = dsc("rk%d" % rb, skb2[:, 0, :],
                      sk_sp[hp * 256:hp * 256 + 128, blk_ * BLK:(blk_ + 1) * BLK])
            e3b = dq("rkh%d" % rb, skb2[:, 1, :],
                     sk_sp[hp * 256 + 128:hp * 256 + 256,
                           blk_ * BLK:(blk_ + 1) * BLK])
            relev[hp] = (e1_, (e2a, e2b), (e3a, e3b), qb, skvb, skb2)
        issue_reload(0, 0)
        issue_reload(1, 0)

        for blk in range(NB):
            v.wait_ge(S["sV"], ev_carryP)
            ev_lndb = inc(v.tensor_scalar_mul(lndb_col[:, :], lnd_col[:, :],
                                              float(blk * BLK)), "sV")
            ev_xre = [None] * 4

            def issue_xre(j):
                x1j = fp(C_X1 + j * D, D)
                gd_x1[j].wait(q, ctx)
                ev_xre[j] = dq("xc%d" % j, x1j,
                               x_in[blk * BLK + j * 128:
                                    blk * BLK + (j + 1) * 128, :])

            ev_op_last = [0]
            ev_op_all = 0
            dst_ = {}
            gacc_q = list(gacc_pend)
            gacc_srcs = list(gacc_srcs_pend)
            gacc_pend = []
            gacc_srcs_pend = []
            inline_ga = [0, 0]

            def den_A(hv):
                hp, hh = hv // 2, hv % 2
                e1_, e2_, e3_, qb, skvb, skb2 = relev[hp]
                rb = hp % 2
                par = hv % 4
                if par == 0:
                    dpow = qe_a[:, :]
                elif par == 1:
                    dpow = qe_b[:, :]
                else:
                    dpow = fp(C_DPW + (par - 2) * 512, 512)
                gd_dpow[par].wait(s, ctx)
                s.wait_ge(S["sV"], ev_lndb)
                ev_dp = inc(s.activation(dpow, tp1_t[:, :], AF.Exp,
                                         scale=lnd_col[:, hv:hv + 1],
                                         bias=lndb_col[:, hv:hv + 1]), "sA")
                stv = fp(C_ST + par * 512, 512)
                gd_st[par].wait(v, ctx)
                v.wait_ge(S["sA"], ev_dp)
                if hh == 0:
                    v.wait_ge(S["rkv%d" % rb], e2_[0])
                else:
                    v.wait_ge(S["rvh%d" % rb], e2_[1])
                v.wait_ge(S["sV"], ev_carryP)
                ev_st1 = inc(v.scalar_tensor_tensor(
                    out=stv, in0=dpow, scalar=carryP[:, hv:hv + 1],
                    in1=skvb[:, hh, :], op0=ALU.mult, op1=ALU.add), "sV")
                stk = fp(C_STK + par * 512, 512)
                gd_stk[par].wait(v, ctx)
                if hh == 0:
                    v.wait_ge(S["rk%d" % rb], e3_[0])
                else:
                    v.wait_ge(S["rkh%d" % rb], e3_[1])
                ev_st2 = inc(v.scalar_tensor_tensor(
                    out=stk, in0=dpow, scalar=carryP[:, H + hv:H + hv + 1],
                    in1=skb2[:, hh, :], op0=ALU.mult, op1=ALU.add), "sV")
                gd_dpow[par].set(("sV", ev_st1), ("sV", ev_st2))
                qsb = sp(C_QS + par * 512, 512)
                gd_qs[par].wait(g, ctx)
                g.wait_ge(S["sV"], ev_st2)
                g.wait_ge(S["rq%d" % rb], e1_)
                ev_qs = inc(g.tensor_mul(qsb, qb[:, hh, :], stk), "sP")
                gd_stk[par].set(("sP", ev_qs))
                dst_[hv] = dict(ev_st1=ev_st1, ev_qs=ev_qs, qb=qb, hh=hh,
                                stv=stv, qsb=qsb, rb=rb)

            def den_B(hv):
                par = hv % 4
                par2 = hv % 2
                d = dst_[hv]
                pd = psD[par2]
                gd_psD[par2].wait(t, ctx)
                t.wait_ge(S["sP"], d["ev_qs"])
                ev_den = inc(t.matmul(pd[0:1, :], onescolbf_t[:, :], d["qsb"],
                                      start=True, stop=True), "sT")
                gd_qs[par].set(("sT", ev_den))
                gd_rows[par2].wait(s, ctx)
                s.wait_ge(S["sT"], ev_den)
                ev_ln = inc(s.activation(lnrows[0:1, par2, :], pd[0:1, :], AF.Ln), "sA")
                s.wait_ge(S["sA"], ev_ln)
                ev_ex = inc(s.activation(denrows[0:1, par2, :], lnrows[0:1, par2, :],
                                         AF.Exp, scale=-1.0), "sA")
                gd_psD[par2].set(("sA", ev_ln))
                rbf = rminbf[0:1, par2, 0:BLK]
                v.wait_ge(S["sA"], ev_ex)
                if prev_bc[par2]:
                    v.wait_ge(S["sT"], prev_bc[par2])
                ev_mn = inc(v.tensor_scalar_min(rbf, denrows[0:1, par2, :], 1e6), "sV")
                d["ev_mn"] = ev_mn
                d["rbf"] = rbf

            def den_C(hv):
                par = hv % 4
                par2 = hv % 2
                d = dst_[hv]
                pb = psA[par]
                gd_psAc[par].wait(t, ctx)
                t.wait_ge(S["sV"], d["ev_mn"])
                ev_bc = inc(t.matmul(pb[:, :], onesrowbf_t[:, :], d["rbf"],
                                     start=True, stop=True), "sT")
                prev_bc[par2] = ev_bc
                gd_rows[par2].set(("sT", ev_bc))
                t2 = fp(C_T2 + par * 512, 512)
                gd_t2[par].wait(v, ctx)
                if par < 2:
                    gd_wsb[par].wait(v, ctx)
                v.wait_ge(S["sV"], d["ev_st1"])
                ev_t2 = inc(v.tensor_mul(t2, d["qb"][:, d["hh"], :], d["stv"]), "sV")
                gd_st[par].set(("sV", ev_t2))
                v.wait_ge(S["sV"], ev_t2)
                v.wait_ge(S["sT"], ev_bc)
                ev_op = inc(v.tensor_mul(outpre[:, hv, :], t2, pb[:, :]), "sV")
                gd_t2[par].set(("sV", ev_op))
                gd_psAc[par].set(("sV", ev_op))
                ev_qs_ = d["ev_qs"]
                del dst_[hv]
                return ev_op, ev_qs_

            for hv in range(H + 2):
                if hv < 4:
                    issue_xre(hv)
                if hv < len(gacc_q):
                    ev9_, slot_ = gacc_q[hv]
                    g.wait_ge(S["sV"], ev9_)
                    g.wait_ge(S["sP"], prev_ga if prev_ga else ev_gz)
                    ev_ga = inc(g.tensor_add(gacc[:, :], gacc[:, :],
                                             gacc_srcs[hv]), "sP")
                    prev_ga = ev_ga
                    ln2_pool_g[slot_] = ev_ga
                if hv < H:
                    den_A(hv)
                if hv >= 1 and hv - 1 < H:
                    den_B(hv - 1)
                if hv >= 2:
                    ho = hv - 2
                    ev_op, ev_qs_ = den_C(ho)
                    ev_op_all = max(ev_op_all, ev_op)
                    if ho % 2 == 1:
                        rb = (ho // 2) % 2
                        gd_qh[rb].set(("sV", ev_op), ("sP", ev_qs_))
                        if ho // 2 + 2 < H // 2:
                            issue_reload(ho // 2 + 2, blk)
                        elif ho // 2 + 2 == H // 2 and blk + 1 < NB:
                            issue_reload(0, blk + 1)
                        elif ho // 2 + 2 == H // 2 + 1 and blk + 1 < NB:
                            issue_reload(1, blk + 1)

            ev_x1sp_blk = []
            for j in range(4):
                x1j = fp(C_X1 + j * D, D)
                slot = j % 2
                mmev = []
                for ocb in range(4):
                    gd_psAc[ocb].wait(t, ctx)
                    if j == 0 and ocb == 0:
                        t.wait_ge(S["wout"], ev_wout)
                        t.wait_ge(S["sV"], ev_op_all)
                    for kc in range(NKC):
                        ins = t.matmul(psA[ocb][:, :],
                                       outpre[:, kc, j * 128:(j + 1) * 128],
                                       woutb2[:, kc, ocb * 512:(ocb + 1) * 512],
                                       start=(kc == 0), stop=(kc == NKC - 1))
                    mmev.append(inc(ins, "sT"))
                st0 = slot * 8
                sumsq = st_stats[:, st0:st0 + 1]
                sumr = st_stats[:, st0 + 1:st0 + 2]
                mean = st_stats[:, st0 + 2:st0 + 3]
                ex2 = st_stats[:, st0 + 3:st0 + 4]
                msq = st_stats[:, st0 + 4:st0 + 5]
                var = st_stats[:, st0 + 5:st0 + 6]
                rstd = st_stats[:, st0 + 6:st0 + 7]
                adds = []
                for ocb in range(4):
                    v.wait_ge(S["sT"], mmev[ocb])
                    v.wait_ge(S["xc%d" % j], ev_xre[j])
                    if stats_g[slot] and ocb == 0:
                        v.wait_ge(S["sV"], stats_g[slot])
                    ins = v.scalar_tensor_tensor(
                        out=x1j[:, ocb * 512:(ocb + 1) * 512],
                        in0=psA[ocb][:, :], scalar=1.0,
                        in1=x1j[:, ocb * 512:(ocb + 1) * 512],
                        op0=ALU.mult, op1=ALU.add,
                        accum_out=st_partials[:, 4 * slot + ocb:4 * slot + ocb + 1])
                    adds.append(inc(ins, "sV"))
                    gd_psAc[ocb].set(("sV", adds[-1]))
                ev_add = adds[-1]
                v.wait_ge(S["sV"], ev_add)
                ev_s1 = inc(v.reduce_sum(sumr, st_partials[:, 4 * slot:4 * slot + 4],
                                         axis=AX.X), "sV")
                gd_scrC.wait(s, ctx)
                s.wait_ge(S["sV"], ev_add)
                ev_sq = inc(s.activation(fp(C_SCR, D), x1j, AF.Square,
                                         accum_out=sumsq), "sA")
                gd_scrC.set(("sA", ev_sq))
                v.wait_ge(S["sA"], ev_sq)
                v.wait_ge(S["sV"], ev_s1)
                inc(v.tensor_scalar_mul(mean, sumr, 1.0 / D), "sV")
                ev3 = inc(v.tensor_scalar_mul(ex2, sumsq, 1.0 / D), "sV")
                v.wait_ge(S["sV"], ev3)
                ev4 = inc(v.tensor_scalar(msq, mean, mean, None, ALU.mult), "sV")
                v.wait_ge(S["sV"], ev4)
                ev5 = inc(v.tensor_sub(var, ex2, msq), "sV")
                s.wait_ge(S["sV"], ev5)
                ev6 = inc(s.activation(msq, var, AF.Ln, bias=EPS_LN), "sA")
                s.wait_ge(S["sA"], ev6)
                ev7 = inc(s.activation(rstd, msq, AF.Exp, scale=-0.5), "sA")
                ln2 = sp(C_LN2 + slot * D, D)
                gd_ln2[slot].wait(v, ctx)
                if j >= 2 and inline_ga[slot]:
                    v.wait_ge(S["sP"], inline_ga[slot])
                elif j < 2 and ln2_pool_g[slot]:
                    v.wait_ge(S["sP"], ln2_pool_g[slot])
                v.wait_ge(S["sA"], ev7)
                ev9 = inc(v.tensor_scalar(ln2, x1j, mean, rstd,
                                          ALU.subtract, ALU.mult), "sV")
                stats_g[slot] = ev9
                if j < 2:
                    g.wait_ge(S["sV"], ev9)
                    g.wait_ge(S["sP"], prev_ga if prev_ga else ev_gz)
                    ev_ga = inc(g.tensor_add(gacc[:, :], gacc[:, :], ln2), "sP")
                    prev_ga = ev_ga
                    inline_ga[slot] = ev_ga
                else:
                    gacc_pend.append((ev9, slot))
                    gacc_srcs_pend.append(ln2)
                q.wait_ge(S["sV"], ev9)
                q.wait_ge(S["sA"], ev_sq)
                ev_x1sp = dq("x1sp%d" % j,
                             x1_sp[blk * BLK + j * 128: blk * BLK + (j + 1) * 128, :], x1j)
                gd_x1[j].set(("x1sp%d" % j, ev_x1sp))
                evt = None
                sl2 = j % 2
                gd_ln2T[sl2].wait(t, ctx)
                for half in range(2):
                    gd_psT[half].wait(t, ctx)
                    t.wait_ge(S["sV"], ev9)
                    for cc in range(8):
                        c = half * 8 + cc
                        ins = t.transpose(psT[half][:, cc * 128:(cc + 1) * 128],
                                          ln2[:, c * 128:(c + 1) * 128], idbf_t[:, :])
                    evt = inc(ins, "sT")
                    v.wait_ge(S["sT"], evt)
                    ins = v.tensor_copy(
                        sp(C_LN2T + sl2 * NKC * 128, NKC * 128)
                        .rearrange("p (c m) -> p c m", c=NKC)[:, half * 8:(half + 1) * 8, :],
                        psT[half][:, :].rearrange("p (c m) -> p c m", c=8))
                    evv = inc(ins, "sV")
                    gd_psT[half].set(("sV", evv))
                gd_ln2[slot].set(("sT", evt))
                q.wait_ge(S["sV"], evv)
                ev_l2sp = dq("l2sp%d" % sl2,
                             ln2T_sp[:, blk * BLK + j * 128: blk * BLK + (j + 1) * 128]
                             .rearrange("(c p) t -> p c t", p=128),
                             sp(C_LN2T + sl2 * NKC * 128, NKC * 128)
                             .rearrange("p (c m) -> p c m", c=NKC))
                gd_ln2T[sl2].set(("l2sp%d" % sl2, ev_l2sp))
            gd_outpre.set(("sT", max(mmev)))

        for i_, (ev9_, _s) in enumerate(gacc_pend):
            g.wait_ge(S["sV"], ev9_)
            g.wait_ge(S["sP"], prev_ga if prev_ga else ev_gz)
            prev_ga = inc(g.tensor_add(gacc[:, :], gacc[:, :],
                                       gacc_srcs_pend[i_]), "sP")

        ctx.barrier()
        for c4 in range(4):
            par = c4 % 2
            gd_psD[par].wait(t, ctx)
            ev_gr = inc(t.matmul(psD[par][0:1, :], onescolf_t[:, :],
                                 gacc[:, c4 * 512:(c4 + 1) * 512],
                                 start=True, stop=True), "sT")
            v.wait_ge(S["sT"], ev_gr)
            ev_gc = inc(v.tensor_copy(grow[0:1, c4 * 512:(c4 + 1) * 512],
                                      psD[par][0:1, :]), "sV")
            gd_psD[par].set(("sV", ev_gc))
        g.wait_ge(S["sV"], ev_gc)
        ev_gin = dg("gin", g_in[:, :], grow[:, :])
        g.wait_ge(S["gin"], ev_gin)
        ev_cc2 = inc(g.collective_compute(
            "AllGather", ALU.bypass, replica_groups=[list(range(NCORES))],
            ins=[g_in.opt()], outs=[g_out.opt()]), "cc2")
        q.wait_ge(S["cc2"], ev_cc2)
        ev_go = dq("gout", gout_sb[:, :], g_out[:, :])
        for c4 in range(4):
            par = c4 % 2
            gd_psD[par].wait(t, ctx)
            t.wait_ge(S["gout"], ev_go)
            ev_ps = inc(t.matmul(psD[par][0:1, :], gpm_t[:, :],
                                 gout_sb[:, c4 * 512:(c4 + 1) * 512],
                                 start=True, stop=True), "sT")
            v.wait_ge(S["sT"], ev_ps)
            v.wait_ge(S["gin"], ev_gin)
            ev_mr = inc(v.tensor_scalar_mul(grow[0:1, c4 * 512:(c4 + 1) * 512],
                                            psD[par][0:1, :], 1.0 / TOT_SEQ), "sV")
            gd_psD[par].set(("sV", ev_mr))
        ev_gw = dq("cst", gWsb[:, :], gateW[:, :])
        gd_psAc[0].wait(t, ctx)
        t.wait_ge(S["sV"], ev_mr)
        for kc in range(NKC):
            ins = t.transpose(psA[0][:, kc:kc + 1],
                              grow[0:1, kc * 128:(kc + 1) * 128], idf_t[0:1, 0:1])
        ev_mt = inc(ins, "sT")
        v.wait_ge(S["sT"], ev_mt)
        ev_mtc = inc(v.tensor_copy(meanT[:, :], psA[0][:, 0:NKC]), "sV")
        gd_psAc[0].set(("sV", ev_mtc))
        gd_psAc[1].wait(t, ctx)
        t.wait_ge(S["cst"], ev_gw)
        for kc in range(NKC):
            ins = t.transpose(psA[1][:, kc * NE:(kc + 1) * NE],
                              gWsb[:, kc * 128:(kc + 1) * 128], idf_t[0:NE, 0:NE])
        ev_gt = inc(ins, "sT")
        v.wait_ge(S["sT"], ev_gt)
        ev_gtc = inc(v.tensor_copy(gWT[:, :], psA[1][:, 0:NKC * NE]), "sV")
        gd_psAc[1].set(("sV", ev_gtc))
        gd_psD[0].wait(t, ctx)
        t.wait_ge(S["sV"], ev_mtc)
        t.wait_ge(S["sV"], ev_gtc)
        for kc in range(NKC):
            ins = t.matmul(psD[0][0:1, 0:NE], meanT[:, kc:kc + 1],
                           gWT[:, kc * NE:(kc + 1) * NE],
                           start=(kc == 0), stop=(kc == NKC - 1))
        ev_lg = inc(ins, "sT")
        v.wait_ge(S["sT"], ev_lg)
        ev_lgc = inc(v.tensor_copy(lgrow[:, :], psD[0][0:1, 0:NE]), "sV")
        gd_psD[0].set(("sV", ev_lgc))
        gd_psD[1].wait(t, ctx)
        t.wait_ge(S["sV"], ev_lgc)
        ev_bc6 = inc(t.matmul(psD[1][:, 0:NE], onesrowf_t[:, :], lgrow[:, :],
                              start=True, stop=True), "sT")
        v.wait_ge(S["sT"], ev_bc6)
        ev = inc(v.tensor_copy(gG[:, :], psD[1][:, 0:NE]), "sV")
        gd_psD[1].set(("sV", ev))
        v.wait_ge(S["sV"], ev)
        ev = inc(v.reduce_max(gm1[:, 0:1], gG[:, :], axis=AX.X), "sV")
        v.wait_ge(S["sV"], ev)
        ev = inc(v.tensor_scalar(gneg[:, 0:1], gm1[:, 0:1], -1.0, None, ALU.mult), "sV")
        s.wait_ge(S["sV"], ev)
        ev = inc(s.activation(gE[:, :], gG[:, :], AF.Exp, bias=gneg[:, 0:1]), "sA")
        v.wait_ge(S["sA"], ev)
        ev = inc(v.reduce_sum(gsum[:, 0:1], gE[:, :], axis=AX.X), "sV")
        v.wait_ge(S["sV"], ev)
        ev = inc(v.reciprocal(grv[:, 0:1], gsum[:, 0:1]), "sV")
        v.wait_ge(S["sV"], ev)
        ev = inc(v.tensor_scalar_mul(gP[:, :], gE[:, :], grv[:, 0:1]), "sV")
        v.wait_ge(S["sV"], ev)
        ev = inc(v.reduce_max(gm1[:, 0:1], gP[:, :], axis=AX.X), "sV")
        v.wait_ge(S["sV"], ev)
        ev = inc(v.tensor_scalar(gmask1[:, :], gP[:, :], gm1[:, 0:1], None,
                                 ALU.is_equal), "sV")
        v.wait_ge(S["sV"], ev)
        ev = inc(v.tensor_scalar_mul(gtmp[:, :], gmask1[:, :], gm1[:, 0:1]), "sV")
        v.wait_ge(S["sV"], ev)
        ev = inc(v.tensor_sub(gp2[:, :], gP[:, :], gtmp[:, :]), "sV")
        v.wait_ge(S["sV"], ev)
        ev = inc(v.reduce_max(gm2[:, 0:1], gp2[:, :], axis=AX.X), "sV")
        v.wait_ge(S["sV"], ev)
        ev = inc(v.tensor_scalar(gmask2[:, :], gp2[:, :], gm2[:, 0:1], None,
                                 ALU.is_equal), "sV")
        v.wait_ge(S["sV"], ev)
        ev = inc(v.tensor_add(gvs[:, 0:1], gm1[:, 0:1], gm2[:, 0:1]), "sV")
        v.wait_ge(S["sV"], ev)
        ev = inc(v.reciprocal(grv[:, 0:1], gvs[:, 0:1]), "sV")
        v.wait_ge(S["sV"], ev)
        ev = inc(v.tensor_mul(gw1[:, 0:1], gm1[:, 0:1], grv[:, 0:1]), "sV")
        v.wait_ge(S["sV"], ev)
        ev_w2c = inc(v.tensor_mul(gw2[:, 0:1], gm2[:, 0:1], grv[:, 0:1]), "sV")
        v.wait_ge(S["sV"], ev_w2c)
        ev = inc(v.scalar_tensor_tensor(out=gtmp[:, :], in0=gmask1[:, :], scalar=1.0,
                                        in1=iota6_t[:, :], op0=ALU.mult, op1=ALU.mult,
                                        accum_out=geidx[0][:, 0:1]), "sV")
        v.wait_ge(S["sV"], ev)
        ev_idx = inc(v.scalar_tensor_tensor(out=gtmp[:, :], in0=gmask2[:, :], scalar=1.0,
                                            in1=iota6_t[:, :], op0=ALU.mult, op1=ALU.mult,
                                            accum_out=geidx[1][:, 0:1]), "sV")
        for sl in range(2):
            v.wait_ge(S["sV"], ev_idx)
            ev = inc(v.tensor_scalar_mul(gesc[sl][:, 0:1], geidx[sl][:, 0:1],
                                         float(D)), "sV")
            v.wait_ge(S["sV"], ev)
            ev = inc(v.tensor_scalar(idxf[:, :], iotaW1_t[:, :], gesc[sl][:, 0:1],
                                     None, ALU.add), "sV")
            v.wait_ge(S["sV"], ev)
            ev = inc(v.tensor_copy(idxi1[sl][:, :], idxf[:, :]), "sV")
            v.wait_ge(S["sV"], ev)
            ev = inc(v.tensor_scalar_mul(gesc[sl][:, 0:1], geidx[sl][:, 0:1],
                                         float(FF)), "sV")
            v.wait_ge(S["sV"], ev)
            ev = inc(v.tensor_scalar(idxf2[:, :], iotaW2_t[:, :], gesc[sl][:, 0:1],
                                     None, ALU.add), "sV")
            v.wait_ge(S["sV"], ev)
            ev = inc(v.tensor_copy(idxi2[sl][:, :], idxf2[:, :]), "sV")
            v.wait_ge(S["sV"], ev)
            ev = inc(v.tensor_copy(idxbi[:, sl:sl + 1], geidx[sl][:, 0:1]), "sV")
        ev_idx_done = ev
        ev_b1c = 0
        for sl in range(2):
            g.wait_ge(S["sV"], ev_idx_done)
            if sl > 0:
                g.wait_ge(S["sV"], ev_b1c)
            ev_b1 = inc(g.indirect_dma_start(
                out=wgather[:, 0:FF], out_offset=None, in_=b1d[:, :],
                in_offset=IndirectOffsetOnAxis(idxbi[:, sl:sl + 1], 0)), "bg", 16)
            v.wait_ge(S["bg"], ev_b1)
            ev_b1c = inc(v.tensor_copy(b1row[0:1, sl, :], wgather[0:1, 0:FF]), "sV")
        g.wait_ge(S["sV"], ev_b1c)
        ev_b2 = inc(g.indirect_dma_start(
            out=wgather[:, 0:D], out_offset=None, in_=b2d[:, :],
            in_offset=IndirectOffsetOnAxis(idxbi[:, 0:1], 0)), "bg", 16)
        v.wait_ge(S["bg"], ev_b2)
        ev = inc(v.tensor_scalar_mul(grow[0:1, :], wgather[0:1, 0:D], gw1[0:1, 0:1]), "sV")
        g.wait_ge(S["sV"], ev)
        ev_b2b = inc(g.indirect_dma_start(
            out=wgather[:, 0:D], out_offset=None, in_=b2d[:, :],
            in_offset=IndirectOffsetOnAxis(idxbi[:, 1:2], 0)), "bg", 16)
        v.wait_ge(S["bg"], ev_b2b)
        v.wait_ge(S["sV"], ev)
        ev = inc(v.scalar_tensor_tensor(out=gacc[0:1, 0:D], in0=wgather[0:1, 0:D],
                                        scalar=gw2[0:1, 0:1], in1=grow[0:1, :],
                                        op0=ALU.mult, op1=ALU.add), "sV")
        v.wait_ge(S["sV"], ev)
        ev_b2c = inc(v.tensor_copy(b2crow[0:1, :], gacc[0:1, 0:D]), "sV")

        ctx.barrier()
        W1s = bigbf[:, 0:NKC * FF].rearrange("p (c f) -> p c f", c=NKC)
        W1flat = W1d.rearrange("e c f -> (e c) f")
        W2flat = W2d.rearrange("e f c -> (e f) c")
        wstage = fpool[:, 8576:8576 + 704].bitcast(BF16)
        wbg = [fpool[:, 14336 + i_ * FF:14336 + (i_ + 1) * FF] for i_ in range(4)]
        wbg2 = [fpool[:, 19968 + i_ * 640:19968 + (i_ + 1) * 640] for i_ in range(2)]
        gd_bg2 = [Gd(), Gd()]
        gd_wgs = [Gd(), Gd()]
        gd_bg = [Gd(), Gd(), Gd(), Gd()]
        gd_stage = Gd()
        gd_hst = Gd()
        gd_ln2Te = Gd()
        gd_psAe = [Gd() for _ in range(4)]
        for b_ in range(4):
            gd_psAe[b_].set(("sV", ctx.C["sV"]), ("sA", ctx.C["sA"]))

        def bg_gather(ti):
            wg = wbg[ti % 4]
            gd_bg[ti % 4].wait(g, ctx)
            g.wait_ge(S["sV"], ev_idx_done)
            if ti < NKC:
                sl, c = 1, ti
                ev = inc(g.indirect_dma_start(
                    out=wg[:, 0:FF], out_offset=None, in_=W1flat,
                    in_offset=IndirectOffsetOnAxis(idxi1[sl][:, c:c + 1], 0)),
                    "bgg%d" % (ti % 4), 16)
            else:
                t2i = ti - NKC
                sl, c = t2i // NFFC, t2i % NFFC
                gd_bg2[ti % 2].wait(g, ctx)
                ev = inc(g.indirect_dma_start(
                    out=wg[:, 0:1408], out_offset=None, in_=W2flat,
                    in_offset=IndirectOffsetOnAxis(idxi2[sl][:, c:c + 1], 0)),
                    "bgg%d" % (ti % 4), 16)
                ev = inc(g.indirect_dma_start(
                    out=fpool[:, 19968 + (ti % 4) * 384:19968 + (ti % 4) * 384 + 384]
                    if False else wbg2[ti % 2], out_offset=None, in_=W2flat,
                    in_offset=IndirectOffsetOnAxis(idxi2[sl][:, c:c + 1], 0),
                    element_offset=1408), "bgg%d" % (ti % 4), 16)
            bg_gev[ti] = ev

        def bg_convert(ti):
            if ti < NKC:
                sl, c = 1, ti
                wg = wbg[ti % 4]
                v.wait_ge(S["bgg%d" % (ti % 4)], bg_gev[ti])
                gd_stage.wait(v, ctx)
                ev_c1 = inc(v.tensor_copy(wstage[:, 0:FF], wg[:, 0:FF]), "sV")
                gd_bg[ti % 4].set(("sV", ev_c1))
                q.wait_ge(S["sV"], ev_c1)
                ev_st = dq("wstg", w1s_bf[c * 128:(c + 1) * 128, :], wstage[:, 0:FF])
                gd_stage.set(("wstg", ev_st))
            else:
                t2i = ti - NKC
                sl, c = t2i // NFFC, t2i % NFFC
                wg = wbg[ti % 4]
                v.wait_ge(S["bgg%d" % (ti % 4)], bg_gev[ti])
                gd_stage.wait(v, ctx)
                ev_c1 = inc(v.tensor_scalar_mul(wstage[:, 0:1408], wg[:, 0:1408],
                                                (gw1 if sl == 0 else gw2)[:, 0:1]), "sV")
                q.wait_ge(S["sV"], ev_c1)
                ev_s1 = dq("wstg", w2s_bf[sl, c * 128:(c + 1) * 128, 0:1408],
                           wstage[:, 0:1408])
                v.wait_ge(S["wstg"], ev_s1)
                ev_c2 = inc(v.tensor_scalar_mul(wstage[:, 0:640], wbg2[ti % 2][:, 0:640],
                                                (gw1 if sl == 0 else gw2)[:, 0:1]), "sV")
                gd_bg[ti % 4].set(("sV", ev_c2))
                gd_bg2[ti % 2].set(("sV", ev_c2))
                q.wait_ge(S["sV"], ev_c2)
                ev_st = dq("wstg", w2s_bf[sl, c * 128:(c + 1) * 128, 1408:2048],
                           wstage[:, 0:640])
                gd_stage.set(("wstg", ev_st))
            bg_done[ti] = ev_st

        NT_BG = NKC + 2 * NFFC
        bg_done = [0] * NT_BG
        bg_gev = [0] * NT_BG
        bg_gi = 0
        bg_i = 0

        w1evs = []
        for c in range(NKC):
            ws = c % 2
            wg = wgather[:, 0:FF] if ws == 0 else fpool[:, E_OUT:E_OUT + FF]
            gd_wgs[ws].wait(g, ctx)
            g.wait_ge(S["sV"], ev_idx_done)
            ev_ga_ = inc(g.indirect_dma_start(
                out=wg, out_offset=None, in_=W1flat,
                in_offset=IndirectOffsetOnAxis(idxi1[0][:, c:c + 1], 0)), "w1g%d" % ws, 16)
            eng = v if ws == 0 else s
            ekey = "sV" if ws == 0 else "sA"
            eng.wait_ge(S["w1g%d" % ws], ev_ga_)
            if ws == 0:
                ev_cv = inc(v.tensor_copy(W1s[:, c, :], wg), "sV")
            else:
                ev_cv = inc(s.activation(W1s[:, c, :], wg, AF.Copy), "sA")
            gd_wgs[ws].set((ekey, ev_cv))
            w1evs.append((ekey, ev_cv))

        hstg2 = [sp(E_HST + i_ * 1024, 1024).rearrange("p (c t) -> p c t", c=2)
                 for i_ in range(2)]
        ln2Tb2 = [sp(E_LN2T + i_ * NKC * 512, NKC * 512)
                  .rearrange("p (c t) -> p c t", c=NKC) for i_ in range(2)]
        gd_l2r = [Gd(), Gd()]
        gd_hst2 = [Gd(), Gd()]
        for sl in range(2):
            if sl == 1:
                while bg_i < NKC:
                    if bg_gi < NKC and bg_gi < bg_i + 4:
                        bg_gather(bg_gi)
                        bg_gi += 1
                    else:
                        bg_convert(bg_i)
                        bg_i += 1
                q.wait_ge(S["wstg"], max(bg_done[0:NKC]))
                t.wait_ge(S["sT"], w1_rd_ev)
                q.wait_ge(S["sT"], w1_rd_ev)
                ev_w1l = dq("wld", W1s[:, :, :],
                            w1s_bf.rearrange("(c p) f -> p c f", p=128))
                t.wait_ge(S["wld"], ev_w1l)
            for blk in range(NB):
                bslot = blk % 2
                ln2Tb = ln2Tb2[bslot]
                gd_l2r[bslot].wait(s, ctx)
                ev_l2 = dsc("l2r%d" % bslot, ln2Tb[:, :, :],
                            ln2T_sp[:, blk * BLK:(blk + 1) * BLK]
                            .rearrange("(c p) t -> p c t", p=128))
                for ffc in range(NFFC):
                    bank = ffc % 2
                    hs = (ffc // 2) % 2
                    gd_psAe[bank].wait(t, ctx)
                    t.wait_ge(S["l2r%d" % bslot], ev_l2)
                    t.wait_ge(S["sV"], ev_b1c)
                    for kc in range(NKC):
                        if sl == 0 and blk == 0 and ffc == 0:
                            ek_, ev_ = w1evs[kc]
                            t.wait_ge(S[ek_], ev_)
                        t.matmul(psA[bank][:, :], W1s[:, kc, ffc * 128:(ffc + 1) * 128],
                                 ln2Tb[:, kc, :], start=(kc == 0), stop=False)
                    ins = t.matmul(psA[bank][:, :], b1row[0:1, sl, ffc * 128:(ffc + 1) * 128],
                                   ones512bf_t[:, :], start=False, stop=True)
                    ev_mm = inc(ins, "sT")
                    s.wait_ge(S["sT"], ev_mm)
                    if ffc % 2 == 0:
                        gd_hst2[hs].wait(s, ctx)
                    ev_h = inc(s.activation(hstg2[hs][:, ffc % 2, :],
                                            psA[bank][:, :], AF.Gelu), "sA")
                    gd_psAe[bank].set(("sA", ev_h))
                    if ffc % 2 == 1 or ffc == NFFC - 1:
                        nff = 2 if ffc % 2 == 1 else 1
                        q.wait_ge(S["sA"], ev_h)
                        ev_hsp = dq("hsp%d" % hs,
                                    h_sp[sl, (ffc - nff + 1) * 128:(ffc + 1) * 128,
                                         blk * BLK:(blk + 1) * BLK]
                                    .rearrange("(c p) t -> p c t", p=128),
                                    hstg2[hs][:, 0:nff, :])
                        gd_hst2[hs].set(("hsp%d" % hs, ev_hsp))
                gd_l2r[bslot].set(("sT", ev_mm))
                nbg = 3 if sl == 0 else 4
                for _ in range(nbg):
                    lead = 4 if bg_gi < NKC else 2
                    if bg_gi < NT_BG and bg_gi < bg_i + lead:
                        bg_gather(bg_gi)
                        bg_gi += 1
                for _ in range(nbg):
                    if bg_i < bg_gi:
                        bg_convert(bg_i)
                        bg_i += 1
            w1_rd_ev = ctx.C["sT"]
        while bg_i < NT_BG:
            lead = 4 if bg_gi < NKC else 2
            if bg_gi < NT_BG and bg_gi < bg_i + lead:
                bg_gather(bg_gi)
                bg_gi += 1
            else:
                bg_convert(bg_i)
                bg_i += 1
        ev_w2stg = ctx.C["wstg"]

        ctx.barrier()
        W2s = bigbf[:, 0:2 * NFFC * 1024].rearrange("p (s c m) -> p s c m", s=2, c=NFFC)
        gd_hb = [Gd(), Gd()]
        gd_x1e = [Gd(), Gd()]
        gd_oute = [Gd(), Gd()]
        gd_psAe2 = [Gd() for _ in range(4)]
        for b_ in range(4):
            gd_psAe2[b_].set(("sA", ctx.C["sA"]), ("sV", ctx.C["sV"]))
        w2_rd_prev = 0
        for ochalf in range(2):
            if ochalf > 0:
                q.wait_ge(S["sT"], w2_rd_prev)
            w2l = []
            q.wait_ge(S["wstg"], ev_w2stg)
            for sl in range(2):
                ev_ = dq("wld", W2s[:, sl, :, :],
                         w2s_bf[sl].rearrange("(c p) m -> p c m", p=128)
                         [:, :, ochalf * 1024:(ochalf + 1) * 1024])
                w2l.append(ev_)
            ev_w2 = max(w2l)
            w2_rd_ev = 0
            for blk in range(NB):
                for jt in range(4):
                    pp = jt % 2
                    x1b = fp(E_X1 + pp * D, D)
                    gd_x1e[pp].wait(q, ctx)
                    ev_x1 = dq("x1r%d" % pp, x1b,
                               x1_sp[blk * BLK + jt * 128: blk * BLK + (jt + 1) * 128, :])
                    hb = sp(E_HBF + pp * 2 * NFFC * 128, 2 * NFFC * 128).rearrange(
                        "p (s c m) -> p s c m", s=2, c=NFFC)
                    gd_hb[pp].wait(s, ctx)
                    ev_h = dsc("hr%d" % pp, hb[:, :, :, :],
                               h_sp[:, :, blk * BLK + jt * 128: blk * BLK + (jt + 1) * 128]
                               .rearrange("s (c p) t -> p s c t", p=128))
                    outb = fp(E_OUT + pp * 1024, 1024)
                    oev = []
                    for oc2 in range(2):
                        gd_psAe2[2 + oc2].wait(t, ctx)
                    t.wait_ge(S["wld"], ev_w2)
                    t.wait_ge(S["hr%d" % pp], ev_h)
                    t.wait_ge(S["sV"], ev_b2c)
                    first = [True, True]
                    ins2 = [None, None]
                    for sl in range(2):
                        for ffc in range(NFFC):
                            for oc2 in range(2):
                                ins2[oc2] = t.matmul(
                                    psA[2 + oc2][:, :], hb[:, sl, ffc, :],
                                    W2s[:, sl, ffc, oc2 * 512:(oc2 + 1) * 512],
                                    start=first[oc2], stop=False)
                                first[oc2] = False
                    for oc2 in range(2):
                        ins2[oc2] = t.matmul(psA[2 + oc2][:, :], onesrowbf_t[:, :],
                                             b2crow[0:1, ochalf * 1024 + oc2 * 512:
                                                    ochalf * 1024 + (oc2 + 1) * 512],
                                             start=False, stop=True)
                        ev_mm = inc(ins2[oc2], "sT")
                        w2_rd_ev = max(w2_rd_ev, ev_mm)
                        v.wait_ge(S["sT"], ev_mm)
                        v.wait_ge(S["x1r%d" % pp], ev_x1)
                        if oc2 == 0:
                            gd_oute[pp].wait(v, ctx)
                        ins = v.tensor_add(outb[:, oc2 * 512:(oc2 + 1) * 512],
                                           psA[2 + oc2][:, :],
                                           x1b[:, ochalf * 1024 + oc2 * 512:
                                               ochalf * 1024 + (oc2 + 1) * 512])
                        ev_o = inc(ins, "sV")
                        gd_psAe2[2 + oc2].set(("sV", ev_o))
                        oev.append(ev_o)
                    gd_hb[pp].set(("sT", w2_rd_ev))
                    gd_x1e[pp].set(("sV", max(oev)))
                    q.wait_ge(S["sV"], max(oev))
                    ev_out = dq("out%d" % pp,
                                out_d[blk * BLK + jt * 128: blk * BLK + (jt + 1) * 128,
                                      ochalf * 1024:(ochalf + 1) * 1024], outb)
                    gd_oute[pp].set(("out%d" % pp, ev_out))
            w2_rd_prev = w2_rd_ev
        q.wait_ge(S["out0"], ctx.C["out0"])
        q.wait_ge(S["out1"], ctx.C["out1"])

    with nc.Block() as block:
        def mk(real_key):
            def body(eng):
                engines = {k: _Fake() for k in "qsvtg"}
                engines[real_key] = eng
                walk(Ctx(engines, sems))
            return body
        block.sync(mk("q"))
        block.scalar(mk("s"))
        block.vector(mk("v"))
        block.tensor(mk("t"))
        block.gpsimd(mk("g"))

    es.close()
    nc.finalize()
    return nc


_NC_CACHE = {}


def _get_nc(T):
    if T not in _NC_CACHE:
        _NC_CACHE[T] = build_nc(T)
    return _NC_CACHE[T]


def kernel(L0_attn_x, L0_attn_w_qkv, L0_attn_w_out, L0_attn_decay_param,
           L0_ffn_gate_W, L0_ffn_W1, L0_ffn_b1, L0_ffn_W2, L0_ffn_b2):
    x = np.ascontiguousarray(np.asarray(L0_attn_x, dtype=np.float32))
    B, TT, Dm = x.shape
    assert Dm == D and B * TT == NCORES * (TT // 2)
    T = TT // 2
    nc_ = _get_nc(T)

    wq = np.ascontiguousarray(np.asarray(L0_attn_w_qkv, dtype=np.float32))
    wo = np.ascontiguousarray(np.asarray(L0_attn_w_out, dtype=np.float32))
    dec = np.ascontiguousarray(np.asarray(L0_attn_decay_param, dtype=np.float32)).reshape(1, H)
    gw = np.ascontiguousarray(np.asarray(L0_ffn_gate_W, dtype=np.float32))
    w1 = np.ascontiguousarray(np.asarray(L0_ffn_W1, dtype=np.float32))
    bb1 = np.ascontiguousarray(np.asarray(L0_ffn_b1, dtype=np.float32))
    w2 = np.ascontiguousarray(np.asarray(L0_ffn_W2, dtype=np.float32))
    bb2 = np.ascontiguousarray(np.asarray(L0_ffn_b2, dtype=np.float32))

    in_maps = []
    for c in range(NCORES):
        b, half = c // 2, c % 2
        xs = np.ascontiguousarray(x[b, half * T:(half + 1) * T, :])
        pm = np.zeros((128, NCORES), np.float32)
        if half == 1:
            pm[:, c - 1] = 1.0
        gpm = np.zeros((NCORES, 1), np.float32)
        gpm[2 * b, 0] = 1.0
        gpm[2 * b + 1, 0] = 1.0
        in_maps.append({
            "x": xs, "wqkv": wq, "wout": wo, "decay": dec, "gateW": gw,
            "W1": w1, "b1": bb1, "W2": w2, "b2": bb2,
            "pmask": pm, "gpmask": gpm,
        })

    res = run_bass_kernel_spmd(nc_, in_maps, core_ids=list(range(NCORES)))
    outp = np.empty((B, TT, D), dtype=np.float32)
    for c in range(NCORES):
        b, half = c // 2, c % 2
        outp[b, half * T:(half + 1) * T, :] = res.results[c]["out"]
    return outp


# version 16 (final)
# speedup vs baseline: 1.0064x; 1.0064x over previous
import numpy as np
import ml_dtypes
from contextlib import ExitStack

import concourse.bass as bass
import concourse.mybir as mybir
from concourse.bass import IndirectOffsetOnAxis
from concourse.bass_utils import run_bass_kernel_spmd

F32 = mybir.dt.float32
BF16 = mybir.dt.bfloat16
I32 = mybir.dt.int32
AF = mybir.ActivationFunctionType
ALU = mybir.AluOpType
AX = mybir.AxisListType

D = 2048
H = 16
FF = 1408
NE = 6
NKC = D // 128
NFFC = FF // 128
BLK = 512
NCORES = 8
EPS_LN = 1e-5


class _FakeInstr:
    def then_inc(self, *a, **k):
        return self


_FI = _FakeInstr()


class _Fake:
    def __getattr__(self, name):
        return lambda *a, **k: _FI


class Ctx:
    def __init__(self, engines, sems):
        self.q = engines["q"]
        self.s = engines["s"]
        self.v = engines["v"]
        self.t = engines["t"]
        self.g = engines["g"]
        self.sems = sems
        self.C = {k: 0 for k in sems}

    def inc(self, ins, key, n=1):
        self.C[key] += n
        ins.then_inc(self.sems[key], n)
        return self.C[key]

    def dq(self, sem, out, in_, **kw):
        return self.inc(self.q.dma_start(out=out, in_=in_, **kw), sem, 16)

    def dsc(self, sem, out, in_, **kw):
        return self.inc(self.s.dma_start(out=out, in_=in_, **kw), sem, 16)

    def dg(self, sem, out, in_, **kw):
        return self.inc(self.g.dma_start(out=out, in_=in_, **kw), sem, 16)

    def barrier(self):
        snap = dict(self.C)
        own = {"sA": self.s, "sV": self.v, "sT": self.t, "sP": self.g}
        for e in (self.q, self.s, self.v, self.t, self.g):
            for k, val in snap.items():
                if k.startswith("cc") or val == 0:
                    continue
                if own.get(k) is not e:
                    e.wait_ge(self.sems[k], val)


class Gd:

    def __init__(self):
        self.w = []

    def wait(self, eng, ctx):
        for k, v_ in self.w:
            eng.wait_ge(ctx.sems[k], v_)

    def set(self, *pairs):
        self.w = [p for p in pairs if p is not None]


def _mk_consts(nc):
    c = {}
    c["id_bf"] = nc.inline_tensor(np.eye(128, dtype=ml_dtypes.bfloat16), "id_bf").ap()
    c["id_f"] = nc.inline_tensor(np.eye(128, dtype=np.float32), "id_f").ap()
    tp1 = np.tile((np.arange(BLK, dtype=np.float32) + 1.0)[None, :], (128, 1))
    c["tp1"] = nc.inline_tensor(tp1, "tp1").ap()
    c["iota6"] = nc.inline_tensor(
        np.tile(np.arange(6, dtype=np.float32)[None, :], (128, 1)), "iota6").ap()
    iw1 = (np.arange(128, dtype=np.float32)[:, None]
           + 128.0 * np.arange(NKC, dtype=np.float32)[None, :])
    c["iotaW1"] = nc.inline_tensor(iw1, "iotaW1").ap()
    iw2 = (np.arange(128, dtype=np.float32)[:, None]
           + 128.0 * np.arange(NFFC, dtype=np.float32)[None, :])
    c["iotaW2"] = nc.inline_tensor(iw2, "iotaW2").ap()
    c["ones512_bf"] = nc.inline_tensor(np.ones((1, BLK), dtype=ml_dtypes.bfloat16), "ones512_bf").ap()
    c["onescol_bf"] = nc.inline_tensor(np.ones((128, 1), dtype=ml_dtypes.bfloat16), "onescol_bf").ap()
    c["onescol_f"] = nc.inline_tensor(np.ones((128, 1), dtype=np.float32), "onescol_f").ap()
    c["onesrow_f"] = nc.inline_tensor(np.ones((1, 128), dtype=np.float32), "onesrow_f").ap()
    c["onesrow_bf"] = nc.inline_tensor(np.ones((1, 128), dtype=ml_dtypes.bfloat16), "onesrow_bf").ap()
    return c


def build_nc(T):
    assert T % 1024 == 0
    SB = T // 2
    NTS = SB // 128
    NG = SB // BLK
    NB = T // BLK
    TOT_SEQ = 2 * T

    nc = bass.Bass("TRN2", target_bir_lowering=False)

    x_in = nc.dram_tensor("x", [T, D], F32, kind="ExternalInput").ap()
    wqkv = nc.dram_tensor("wqkv", [D, 3 * D], F32, kind="ExternalInput").ap()
    wout_d = nc.dram_tensor("wout", [D, D], F32, kind="ExternalInput").ap()
    decay = nc.dram_tensor("decay", [1, H], F32, kind="ExternalInput").ap()
    gateW = nc.dram_tensor("gateW", [NE, D], F32, kind="ExternalInput").ap()
    W1d = nc.dram_tensor("W1", [NE, D, FF], F32, kind="ExternalInput").ap()
    b1d = nc.dram_tensor("b1", [NE, FF], F32, kind="ExternalInput").ap()
    W2d = nc.dram_tensor("W2", [NE, FF, D], F32, kind="ExternalInput").ap()
    b2d = nc.dram_tensor("b2", [NE, D], F32, kind="ExternalInput").ap()
    pmask = nc.dram_tensor("pmask", [128, NCORES], F32, kind="ExternalInput").ap()
    gpmask = nc.dram_tensor("gpmask", [NCORES, 1], F32, kind="ExternalInput").ap()
    out_d = nc.dram_tensor("out", [T, D], F32, kind="ExternalOutput").ap()

    wqkv_bf = nc.dram_tensor("wqkv_bf", [D, 3 * D], BF16).ap()
    wout_bf = nc.dram_tensor("wout_bf", [D, D], BF16).ap()
    w1s_bf = nc.dram_tensor("w1s_bf", [D, FF], BF16).ap()
    w2s_bf = nc.dram_tensor("w2s_bf", [2, FF, D], BF16).ap()
    q_sp = nc.dram_tensor("q_sp", [D, T], BF16).ap()
    skv_sp = nc.dram_tensor("skv_sp", [D, T], F32).ap()
    sk_sp = nc.dram_tensor("sk_sp", [D, T], F32).ap()
    x1_sp = nc.dram_tensor("x1_sp", [T, D], F32).ap()
    ln2T_sp = nc.dram_tensor("ln2T_sp", [D, T], BF16).ap()
    h_sp = nc.dram_tensor("h_sp", [2, FF, T], BF16).ap()
    ag_in = nc.dram_tensor("ag_in", [128, 2 * H], F32).ap()
    ag_out = nc.dram_tensor("ag_out", [NCORES * 128, 2 * H], F32, addr_space="Shared").ap()
    g_in = nc.dram_tensor("g_in", [1, D], F32).ap()
    g_out = nc.dram_tensor("g_out", [NCORES, D], F32, addr_space="Shared").ap()

    cst = _mk_consts(nc)
    constv_d = nc.inline_tensor(
        np.tile(np.array([[EPS_LN, -0.5, -1.0]], dtype=np.float32), (128, 1)),
        "constv_dram").ap()

    es = ExitStack()
    sb_ = lambda n, s, d: es.enter_context(nc.sbuf_tensor(n, s, d))
    ps_ = lambda n, s, d: es.enter_context(nc.psum_tensor(n, s, d))
    sem_ = lambda n: es.enter_context(nc.semaphore(n))

    BIGBF = max(NKC * SB, NKC * NKC * 128, NKC * FF, 2 * NFFC * 1024)
    SMBF = max(3 * NKC * 256 + 2 * D + 2 * SB,
               NKC * 512 + 2048 + 2048 + 2 * D + 2 * NKC * 128,
               NKC * 512 + NFFC * 512,
               2 * 2 * NFFC * 128)
    F32N = max(3 * D + 5 * SB + 4096,
               5 * D + 2 * 2048 + 2048 + 2048 + 2048 + 1024,
               12288 + D, 19968 + 2 * 640)
    bigbf = sb_("bigbf", [128, BIGBF], BF16)
    smbf = sb_("smbf", [128, SMBF], BF16)
    fpool = sb_("fpool", [128, F32N], F32)

    st_stats = sb_("st_stats", [128, 16], F32)
    st_partials = sb_("st_partials", [128, 8], F32)
    d_row = sb_("d_row", [1, H], F32)
    lnd_row = sb_("lnd_row", [1, H], F32)
    d_col = sb_("d_col", [128, H], F32)
    lnd_col = sb_("lnd_col", [128, H], F32)
    lndb_col = sb_("lndb_col", [128, H], F32)
    carry = sb_("carry", [128, 2 * H], F32)
    carryP = sb_("carryP", [128, 2 * H], F32)
    carryT = sb_("carryT", [128, 2 * H], F32)
    pmask_t = sb_("pmask_t", [128, NCORES], F32)
    gpm_t = sb_("gpm_t", [NCORES, 1], F32)
    idbf_t = sb_("idbf_t", [128, 128], BF16)
    idf_t = sb_("idf_t", [128, 128], F32)
    tp1_t = sb_("tp1_t", [128, BLK], F32)
    iota6_t = sb_("iota6_t", [128, NE], F32)
    iotaW1_t = sb_("iotaW1_t", [128, NKC], F32)
    iotaW2_t = sb_("iotaW2_t", [128, NFFC], F32)
    ones512bf_t = sb_("ones512bf_t", [1, BLK], BF16)
    onescolbf_t = sb_("onescolbf_t", [128, 1], BF16)
    onescolf_t = sb_("onescolf_t", [128, 1], F32)
    onesrowf_t = sb_("onesrowf_t", [1, 128], F32)
    onesrowbf_t = sb_("onesrowbf_t", [1, 128], BF16)
    gacc = sb_("gacc", [128, D], F32)
    gWT = sb_("gWT", [128, NKC * NE], F32)
    meanT = sb_("meanT", [128, NKC], F32)
    lgrow = sb_("lgrow", [1, NE], F32)
    gG = sb_("gG", [128, NE], F32)
    gE = sb_("gE", [128, NE], F32)
    gP = sb_("gP", [128, NE], F32)
    gm1 = sb_("gm1", [128, 1], F32)
    gm2 = sb_("gm2", [128, 1], F32)
    gneg = sb_("gneg", [128, 1], F32)
    gsum = sb_("gsum", [128, 1], F32)
    grv = sb_("grv", [128, 1], F32)
    gmask1 = sb_("gmask1", [128, NE], F32)
    gmask2 = sb_("gmask2", [128, NE], F32)
    gtmp = sb_("gtmp", [128, NE], F32)
    gp2 = sb_("gp2", [128, NE], F32)
    gvs = sb_("gvs", [128, 1], F32)
    gw1 = sb_("gw1", [128, 1], F32)
    gw2 = sb_("gw2", [128, 1], F32)
    geidx = [sb_("geidx%d" % i, [128, 1], F32) for i in range(2)]
    gesc = [sb_("gesc%d" % i, [128, 1], F32) for i in range(2)]
    idxf = sb_("idxf", [128, NKC], F32)
    idxi1 = [sb_("idxi1_%d" % i, [128, NKC], I32) for i in range(2)]
    idxf2 = sb_("idxf2", [128, NFFC], F32)
    idxi2 = [sb_("idxi2_%d" % i, [128, NFFC], I32) for i in range(2)]
    idxbi = sb_("idxbi", [128, 2], I32)
    qe_a = sb_("qe_a", [128, BLK], F32)
    qe_b = sb_("qe_b", [128, BLK], F32)

    constv = sb_("constv", [128, 4], F32)
    nc.const_aps.register(F32, EPS_LN, constv[:, 0:1])
    nc.const_aps.register(F32, -0.5, constv[:, 1:2])
    nc.const_aps.register(F32, -1.0, constv[:, 2:3])

    psA = [ps_("psA%d" % i, [128, 512], F32) for i in range(4)]
    psT = [ps_("psT%d" % i, [128, 1024], BF16) for i in range(2)]
    psD = [ps_("psD%d" % i, [128, 512], F32) for i in range(2)]

    SEMKEYS = (["sV", "sA", "sT", "sP", "cc1", "cc2",
                "cst", "xa0", "xa1", "wq0", "wq1", "wq2", "wq3", "qsp0", "qsp1", "skvsp", "sksp",
                "agi", "ago", "wout", "xc0", "xc1", "xc2", "xc3",
                "rq0", "rq1", "rkv0", "rkv1", "rk0", "rk1",
                "x1sp0", "x1sp1", "x1sp2", "x1sp3", "l2sp0", "l2sp1",
                "gin", "gout", "bg", "w1g0", "w1g1", "rvh0", "rvh1", "rvh2", "rkh0", "rkh1", "rkh2", "rkv2", "rk2", "w2g0", "w2g1", "wcast", "wcast2", "wstg", "wld", "bgg0", "bgg1", "bgg2", "bgg3", "l2r0", "l2r1", "hsp0", "hsp1",
                "x1r0", "x1r1", "hr0", "hr1", "out0", "out1"])
    sems = {k: sem_(k) for k in SEMKEYS}

    A_XT = 0
    A_SCR = 2 * D
    A_PHIK = 3 * D
    A_VH = 3 * D + SB
    A_KVIN = 3 * D + 2 * SB
    A_SKV = 3 * D + 3 * SB
    A_SK = 3 * D + 4 * SB
    C_X1 = 0
    C_SCR = 4 * D
    C_SKVB = 5 * D
    C_SKB = 5 * D + 3072
    C_ST = 5 * D + 6144
    C_STK = C_ST + 1536
    C_T2 = C_STK + 1536
    C_DPW = C_T2 + 1024
    E_X1 = 0
    E_OUT = 2 * D

    A_WQ = 0
    A_LNT = 3 * NKC * 256
    A_QSTG = A_LNT + 2 * D
    C_OUTPRE = 0
    C_QH = NKC * 512
    C_QS = C_QH + 2048
    C_LN2 = C_QS + 2048
    C_LN2T = C_LN2 + 2 * D
    E_LN2T = 0
    E_HST = 2 * NKC * 512
    E_HBF = 0

    def fp(off, n):
        return fpool[:, off:off + n]

    wgather = fpool[:, 12288:12288 + D]
    grow = fpool[0:1, C_SKVB:C_SKVB + D]
    rminbf = fpool[0:1, C_SCR:C_SCR + 2 * BLK].bitcast(BF16).rearrange(
        "p (s t) -> p s t", s=2)
    gWsb = fpool[0:NE, C_SKB:C_SKB + D]
    agbuf = fpool[:, 0:NCORES * 2 * H].rearrange("p (r c) -> p r c", r=NCORES)
    wgather2 = [fpool[:, 12288:12288 + 1024], fpool[:, 12288 + 1024:12288 + D]]
    gout_sb = gacc[0:NCORES, :]
    lnrows = fpool[0:1, C_SCR:C_SCR + 2 * BLK].rearrange("p (s t) -> p s t", s=2)
    denrows = fpool[0:1, C_SCR + 2 * BLK:C_SCR + 4 * BLK].rearrange("p (s t) -> p s t", s=2)
    b1row = fpool[0:1, 6144:6144 + FF].bitcast(BF16).rearrange("p (s f) -> p s f", s=2)
    b2crow = fpool[0:1, 7552:7552 + 1024].bitcast(BF16)

    def sp(off, n):
        return smbf[:, off:off + n]

    def walk(ctx: Ctx):
        q, s, v, t, g = ctx.q, ctx.s, ctx.v, ctx.t, ctx.g
        S = ctx.sems
        inc = ctx.inc
        dq, dsc, dg = ctx.dq, ctx.dsc, ctx.dg

        for dst, src in [
            (pmask_t[:, :], pmask), (gpm_t[:, :], gpmask),
            (idbf_t[:, :], cst["id_bf"]), (idf_t[:, :], cst["id_f"]),
            (tp1_t[:, :], cst["tp1"]), (iota6_t[:, :], cst["iota6"]),
            (iotaW1_t[:, :], cst["iotaW1"]), (iotaW2_t[:, :], cst["iotaW2"]),
            (ones512bf_t[:, :], cst["ones512_bf"]),
            (onescolbf_t[:, :], cst["onescol_bf"]),
            (onescolf_t[:, :], cst["onescol_f"]),
            (onesrowf_t[:, :], cst["onesrow_f"]),
            (onesrowbf_t[:, :], cst["onesrow_bf"]),
            (d_row[:, :], decay), (gWsb[:, :], gateW),
            (constv[:, 0:3], constv_d),
        ]:
            ev_const = dq("cst", dst, src[:, :])
        s.wait_ge(S["cst"], ev_const)
        ev = inc(s.activation(d_row[:, :], d_row[:, :], AF.Sigmoid), "sA")
        s.wait_ge(S["sA"], ev)
        ev_lnd = inc(s.activation(lnd_row[:, :], d_row[:, :], AF.Ln), "sA")
        t.wait_ge(S["sA"], ev_lnd)
        inc(t.matmul(psD[0][:, 0:H], onesrowf_t[:, :], d_row[:, :],
                     start=True, stop=True), "sT")
        ev_t = inc(t.matmul(psD[0][:, H:2 * H], onesrowf_t[:, :], lnd_row[:, :],
                            start=True, stop=True), "sT")
        v.wait_ge(S["sT"], ev_t)
        inc(v.tensor_copy(d_col[:, :], psD[0][:, 0:H]), "sV")
        ev_dcol = inc(v.tensor_copy(lnd_col[:, :], psD[0][:, H:2 * H]), "sV")
        ev_gz = inc(g.memset(gacc[:, :], 0.0), "sP")
        for rc in range(NKC):
            dg("wcast", wqkv_bf[rc * 128:(rc + 1) * 128, :],
               wqkv[rc * 128:(rc + 1) * 128, :])
        ev_wqcast = ctx.C["wcast"]
        for rc in range(NKC):
            dg("wcast2", wout_bf[rc * 128:(rc + 1) * 128, :],
               wout_d[rc * 128:(rc + 1) * 128, :])
        ev_wocast = ctx.C["wcast2"]

        gd_psD = [Gd(), Gd()]
        gd_psD[0].set(("sV", ev_dcol))
        gd_psD[1].set(("sV", ev_dcol))

        gd_xt = [Gd(), Gd()]
        gd_lnt = [Gd(), Gd()]
        gd_psT = [Gd(), Gd()]
        gd_psA = [Gd() for _ in range(6)]
        gd_wq = [Gd(), Gd(), Gd()]
        gd_qstg = [Gd(), Gd()]
        gd_skv = Gd()
        gd_sk = Gd()
        gd_scrA = Gd()
        gd_head = Gd()
        gd_qe = Gd()
        gd_qeg = [Gd() for _ in range(4)]
        carry_ev = {}
        stats_g = [0, 0]
        lnxT = bigbf[:, 0:NKC * SB].rearrange("p (c t) -> p c t", c=NKC)
        wq3 = [sp(A_WQ + i * NKC * 256, NKC * 256)
               .rearrange("p (kc s m) -> p kc s m", kc=NKC, s=2) for i in range(3)]
        lnxT_rd_ev = None

        for sb in range(2):
            ev_lnx_all = 0
            for tt in range(NTS):
                slot = tt % 2
                xt = fp(A_XT + slot * D, D)
                gd_xt[slot].wait(q, ctx)
                ev_x = dq("xa%d" % slot, xt,
                          x_in[sb * SB + tt * 128: sb * SB + (tt + 1) * 128, :])
                st0 = slot * 8
                sumsq = st_stats[:, st0:st0 + 1]
                sumr = st_stats[:, st0 + 1:st0 + 2]
                mean = st_stats[:, st0 + 2:st0 + 3]
                ex2 = st_stats[:, st0 + 3:st0 + 4]
                msq = st_stats[:, st0 + 4:st0 + 5]
                var = st_stats[:, st0 + 5:st0 + 6]
                rstd = st_stats[:, st0 + 6:st0 + 7]
                negmr = st_stats[:, st0 + 7:st0 + 8]
                scr = fp(A_SCR, D)
                v.wait_ge(S["xa%d" % slot], ev_x)
                if stats_g[slot]:
                    v.wait_ge(S["sV"], stats_g[slot])
                ev1 = inc(v.reduce_sum(sumr, xt, axis=AX.X), "sV")
                s.wait_ge(S["xa%d" % slot], ev_x)
                if stats_g[slot]:
                    s.wait_ge(S["sV"], stats_g[slot])
                gd_scrA.wait(s, ctx)
                ev2 = inc(s.activation(scr, xt, AF.Square, accum_out=sumsq), "sA")
                gd_scrA.set(("sA", ev2))
                v.wait_ge(S["sA"], ev2)
                v.wait_ge(S["sV"], ev1)
                inc(v.tensor_scalar_mul(mean, sumr, 1.0 / D), "sV")
                ev3 = inc(v.tensor_scalar_mul(ex2, sumsq, 1.0 / D), "sV")
                v.wait_ge(S["sV"], ev3)
                ev4 = inc(v.tensor_scalar(msq, mean, mean, None, ALU.mult), "sV")
                v.wait_ge(S["sV"], ev4)
                ev5 = inc(v.tensor_sub(var, ex2, msq), "sV")
                s.wait_ge(S["sV"], ev5)
                ev6 = inc(s.activation(msq, var, AF.Ln, bias=EPS_LN), "sA")
                s.wait_ge(S["sA"], ev6)
                ev7 = inc(s.activation(rstd, msq, AF.Exp, scale=-0.5), "sA")
                lnt = sp(A_LNT + slot * D, D)
                gd_lnt[slot].wait(v, ctx)
                v.wait_ge(S["sA"], ev7)
                ev9 = inc(v.tensor_scalar(lnt, xt, mean, rstd,
                                          ALU.subtract, ALU.mult), "sV")
                stats_g[slot] = ev9
                gd_xt[slot].set(("sV", ev9), ("sA", ev2))
                evt = None
                for half in range(2):
                    gd_psT[half].wait(t, ctx)
                    t.wait_ge(S["sV"], ev9)
                    if lnxT_rd_ev is not None:
                        t.wait_ge(S["sT"], lnxT_rd_ev)
                    for cc in range(8):
                        c = half * 8 + cc
                        ins = t.transpose(psT[half][:, cc * 128:(cc + 1) * 128],
                                          lnt[:, c * 128:(c + 1) * 128], idbf_t[:, :])
                    evt = inc(ins, "sT")
                    v.wait_ge(S["sT"], evt)
                    ins = v.tensor_copy(
                        lnxT[:, half * 8:(half + 1) * 8, tt * 128:(tt + 1) * 128],
                        psT[half][:, :].rearrange("p (c m) -> p c m", c=8))
                    evv = inc(ins, "sV")
                    gd_psT[half].set(("sV", evv))
                    ev_lnx_all = max(ev_lnx_all, evv)
                gd_lnt[slot].set(("sT", evt))

            wqv = wqkv_bf.rearrange("(kc p) (g h m) -> p kc g h m", p=128, g=3, m=128)
            pairs = [("kv", h) for h in range(H)] + [("q", j) for j in range(8)]
            banks6 = psA + psD
            rot = 0
            for pi, (kind, j) in enumerate(pairs):
                wslot = pi % 3
                gd_wq[wslot].wait(s, ctx)
                s.wait_ge(S["wcast"], ev_wqcast)
                wqflat = sp(A_WQ + wslot * NKC * 256, NKC * 256).rearrange(
                    "p (kc m) -> p kc m", kc=NKC)
                if kind == "q":
                    dsc("wq%d" % wslot, wqflat[:, :, :],
                        wqkv_bf[:, 2 * j * 128:(2 * j + 2) * 128]
                        .rearrange("(kc p) m -> p kc m", p=128))
                else:
                    dsc("wq%d" % wslot, wq3[wslot][:, :, 0, :], wqv[:, :, 1, j, :])
                    dsc("wq%d" % wslot, wq3[wslot][:, :, 1, :], wqv[:, :, 2, j, :])
                ev_w = ctx.C["wq%d" % wslot]
                ev_mm_last = None
                for s01 in range(2):
                    if kind == "q":
                        oc = 2 * j + s01
                        is_q, is_k = True, False
                    else:
                        oc = (16 + j) if s01 == 0 else (32 + j)
                        is_q, is_k = False, (s01 == 0)
                    hv = j
                    mybanks = [banks6[(rot + i) % 6] for i in range(NG)]
                    mybank_ix = [(rot + i) % 6 for i in range(NG)]
                    rot = (rot + NG) % 6
                    for i in range(NG):
                        gd_psA[mybank_ix[i]].wait(t, ctx)
                    t.wait_ge(S["wq%d" % wslot], ev_w)
                    t.wait_ge(S["sV"], ev_lnx_all)
                    for kc in range(NKC):
                        if kind == "q":
                            lhs = wqflat[:, kc, s01 * 128:(s01 + 1) * 128]
                        else:
                            lhs = wq3[wslot][:, kc, s01, :]
                        for grp in range(NG):
                            ins = t.matmul(mybanks[grp][:, :], lhs,
                                           lnxT[:, kc, grp * 512:(grp + 1) * 512],
                                           start=(kc == 0), stop=(kc == NKC - 1))
                    ev_mm = inc(ins, "sT")
                    ev_mm_last = ev_mm
                    ev_evac_last = None
                    for grp in range(NG):
                        bank = mybanks[grp]
                        bix = mybank_ix[grp]
                        if is_q or is_k:
                            qeA = fp(16384 + grp * 512, 512)
                            qeB = fp(18432 + grp * 512, 512)
                            gd_qeg[grp].wait(s, ctx)
                            s.wait_ge(S["sT"], ev_mm)
                            eva = inc(s.activation(qeA, bank[:, :], AF.Exp), "sA")
                            v.wait_ge(S["sA"], eva)
                            evb = inc(v.tensor_scalar_min(qeB, qeA, 1.0), "sV")
                            if is_q:
                                qsl = (2 * j + s01) % 2
                                dst = sp(A_QSTG + qsl * SB + grp * 512, 512)
                                if grp == 0:
                                    gd_qstg[qsl].wait(v, ctx)
                            else:
                                dst = fp(A_PHIK + grp * 512, 512)
                                if grp == 0:
                                    gd_head.wait(v, ctx)
                            v.wait_ge(S["sV"], evb)
                            v.wait_ge(S["sT"], ev_mm)
                            ev_evac = inc(v.scalar_tensor_tensor(
                                out=dst, in0=bank[:, :], scalar=0.0,
                                in1=qeB, op0=ALU.max, op1=ALU.add), "sV")
                            gd_psA[bix].set(("sV", ev_evac))
                            gd_qeg[grp].set(("sV", ev_evac))
                        else:
                            s.wait_ge(S["sT"], ev_mm)
                            if grp == 0:
                                gd_head.wait(s, ctx)
                            ev_evac = inc(s.activation(fp(A_VH + grp * 512, 512),
                                                       bank[:, :], AF.Copy), "sA")
                            gd_psA[bix].set(("sA", ev_evac))
                        ev_evac_last = ev_evac
                    if is_k:
                        k_done_ev = ev_evac_last
                    if is_q:
                        qsl = oc % 2
                        q.wait_ge(S["sV"], ev_evac_last)
                        ev_sp = dq("qsp%d" % qsl,
                                   q_sp[oc * 128:(oc + 1) * 128, sb * SB:(sb + 1) * SB],
                                   sp(A_QSTG + qsl * SB, SB))
                        gd_qstg[qsl].set(("qsp%d" % qsl, ev_sp))
                    elif not is_k:
                        phik = fp(A_PHIK, SB)
                        vh = fp(A_VH, SB)
                        kvin = fp(A_KVIN, SB)
                        dfull = d_col[:, hv:hv + 1].broadcast_to([128, SB])
                        skv = fp(A_SKV, SB)
                        skb = fp(A_SK, SB)
                        v.wait_ge(S["sA"], ev_evac_last)
                        v.wait_ge(S["sV"], k_done_ev)
                        gd_head.wait(v, ctx)
                        ev_kv = inc(v.tensor_mul(kvin, phik, vh), "sV")
                        v.wait_ge(S["sV"], ev_dcol)
                        gd_skv.wait(v, ctx)
                        init_kv = carry[:, hv:hv + 1] if sb > 0 else 0.0
                        init_k = carry[:, H + hv:H + hv + 1] if sb > 0 else 0.0
                        if sb > 0:
                            v.wait_ge(S["sV"], carry_ev[("kv", hv)])
                            v.wait_ge(S["sV"], carry_ev[("k", hv)])
                        v.wait_ge(S["sV"], ev_kv)
                        ev_skv = inc(v.tensor_tensor_scan(skv, dfull, kvin, init_kv,
                                                          ALU.mult, ALU.add), "sV")
                        gd_sk.wait(v, ctx)
                        ev_sk = inc(v.tensor_tensor_scan(skb, dfull, phik, init_k,
                                                         ALU.mult, ALU.add), "sV")
                        v.wait_ge(S["sV"], ev_skv)
                        ev_ckv = inc(v.tensor_copy(carry[:, hv:hv + 1],
                                                   skv[:, SB - 1:SB]), "sV")
                        v.wait_ge(S["sV"], ev_sk)
                        ev_ck = inc(v.tensor_copy(carry[:, H + hv:H + hv + 1],
                                                  skb[:, SB - 1:SB]), "sV")
                        carry_ev[("kv", hv)] = ev_ckv
                        carry_ev[("k", hv)] = ev_ck
                        q.wait_ge(S["sV"], ev_ckv)
                        e1_ = dq("skvsp", skv_sp[hv * 128:(hv + 1) * 128,
                                                 sb * SB:(sb + 1) * SB], skv)
                        q.wait_ge(S["sV"], ev_ck)
                        e2_ = dq("sksp", sk_sp[hv * 128:(hv + 1) * 128,
                                               sb * SB:(sb + 1) * SB], skb)
                        gd_skv.set(("skvsp", e1_))
                        gd_sk.set(("sksp", e2_))
                        gd_head.set(("sV", ev_sk), ("sV", ev_skv))
                gd_wq[wslot].set(("sT", ev_mm_last))
            lnxT_rd_ev = ctx.C["sT"]

        ev_c_all = max(carry_ev.values())
        g.wait_ge(S["sV"], ev_c_all)
        ev_agin = dg("agi", ag_in[:, :], carry[:, :])
        g.wait_ge(S["agi"], ev_agin)
        ev_cc1 = inc(g.collective_compute(
            "AllGather", ALU.bypass, replica_groups=[list(range(NCORES))],
            ins=[ag_in.opt()], outs=[ag_out.opt()]), "cc1")
        q.wait_ge(S["cc1"], ev_cc1)
        ev_ag = dq("ago", agbuf[:, :, :], ag_out.rearrange("(r p) c -> p r c", p=128))
        v.wait_ge(S["ago"], ev_ag)
        ev_cp = inc(v.tensor_scalar_mul(carryT[:, :], agbuf[:, 0, :],
                                        pmask_t[:, 0:1]), "sV")
        for r in range(1, NCORES):
            src, dst = (carryT, carryP) if r % 2 == 1 else (carryP, carryT)
            v.wait_ge(S["sV"], ev_cp)
            ev_cp = inc(v.scalar_tensor_tensor(
                out=dst[:, :], in0=agbuf[:, r, :], scalar=pmask_t[:, r:r + 1],
                in1=src[:, :], op0=ALU.mult, op1=ALU.add), "sV")
        ev_carryP = ev_cp

        woutb = bigbf[:, 0:NKC * NKC * 128].rearrange(
            "p (kc oc m) -> p kc oc m", kc=NKC, oc=NKC)
        woutb2 = bigbf[:, 0:NKC * NKC * 128].rearrange(
            "p (kc ocm) -> p kc ocm", kc=NKC)
        q.wait_ge(S["sT"], lnxT_rd_ev)
        q.wait_ge(S["wcast2"], ev_wocast)
        for kc in range(NKC):
            ev_wout = dq("wout", woutb[:, kc, :, :],
                         wout_bf[kc * 128:(kc + 1) * 128, :]
                         .rearrange("p (oc m) -> p oc m", oc=NKC))
        ctx.barrier()
        outpre = sp(C_OUTPRE, NKC * 512).rearrange("p (c t) -> p c t", c=NKC)
        gd_x1 = [Gd() for _ in range(4)]
        gd_qh = [Gd(), Gd()]
        gd_psAc = [Gd() for _ in range(4)]
        gd_dpow = [Gd() for _ in range(4)]
        gd_st = [Gd() for _ in range(3)]
        gd_stk = [Gd() for _ in range(3)]
        gd_t2 = [Gd(), Gd()]
        gd_qs = [Gd() for _ in range(4)]
        gd_skv = [Gd() for _ in range(3)]
        gd_sk = [Gd() for _ in range(3)]
        gd_rows = [Gd(), Gd()]
        gd_outpre = Gd()
        prev_ga = 0
        gacc_pend = []
        gacc_srcs_pend = []
        gacc_srcs = []
        ln2_pool_g = [0, 0]
        prev_bc = [0, 0]
        gd_ln2 = [Gd(), Gd()]
        gd_ln2T = [Gd(), Gd()]
        gd_scrC = Gd()
        for b_ in range(4):
            gd_psAc[b_].set(("sA", ctx.C["sA"]), ("sV", ctx.C["sV"]))
        gd_psD[0].set(("sA", ctx.C["sA"]), ("sV", ctx.C["sV"]))
        gd_psD[1].set(("sA", ctx.C["sA"]), ("sV", ctx.C["sV"]))

        rel_q = {}
        rel_kv = {}
        rel_k = {}

        def issue_q(hp, blk_):
            rb = hp % 2
            gd_qh[rb].wait(s, ctx)
            qb = sp(C_QH + rb * 1024, 1024).rearrange("p (hh t) -> p hh t", hh=2)
            e1_ = dsc("rq%d" % rb, qb[:, :, :],
                      q_sp[hp * 256:(hp + 1) * 256, blk_ * BLK:(blk_ + 1) * BLK]
                      .rearrange("(hh p) t -> p hh t", p=128))
            rel_q[rb] = (e1_, qb)

        def issue_kv(hp, blk_):
            r3 = (blk_ * (H // 2) + hp) % 3
            gd_skv[r3].wait(q, ctx)
            gd_skv[r3].wait(s, ctx)
            gd_sk[r3].wait(q, ctx)
            gd_sk[r3].wait(s, ctx)
            skvb = fp(C_SKVB + r3 * 1024, 1024).rearrange("p (hh t) -> p hh t", hh=2)
            skb2 = fp(C_SKB + r3 * 1024, 1024).rearrange("p (hh t) -> p hh t", hh=2)
            e2a = dq("rkv%d" % r3, skvb[:, 0, :],
                     skv_sp[hp * 256:hp * 256 + 128, blk_ * BLK:(blk_ + 1) * BLK])
            e2b = dsc("rvh%d" % r3, skvb[:, 1, :],
                      skv_sp[hp * 256 + 128:hp * 256 + 256,
                             blk_ * BLK:(blk_ + 1) * BLK])
            e3a = dsc("rk%d" % r3, skb2[:, 0, :],
                      sk_sp[hp * 256:hp * 256 + 128, blk_ * BLK:(blk_ + 1) * BLK])
            e3b = dq("rkh%d" % r3, skb2[:, 1, :],
                     sk_sp[hp * 256 + 128:hp * 256 + 256,
                           blk_ * BLK:(blk_ + 1) * BLK])
            rel_kv[r3] = ((e2a, e2b), skvb)
            rel_k[r3] = ((e3a, e3b), skb2)
        for p_ in range(3):
            issue_kv(p_, 0)
        issue_q(0, 0)
        issue_q(1, 0)

        for blk in range(NB):
            v.wait_ge(S["sV"], ev_carryP)
            ev_lndb = inc(v.tensor_scalar_mul(lndb_col[:, :], lnd_col[:, :],
                                              float(blk * BLK)), "sV")
            ev_xre = [None] * 4

            def issue_xre(j):
                x1j = fp(C_X1 + j * D, D)
                gd_x1[j].wait(q, ctx)
                ev_xre[j] = dq("xc%d" % j, x1j,
                               x_in[blk * BLK + j * 128:
                                    blk * BLK + (j + 1) * 128, :])

            ev_op_last = [0]
            ev_op_all = 0
            dst_ = {}
            gacc_q = list(gacc_pend)
            gacc_srcs = list(gacc_srcs_pend)
            gacc_pend = []
            gacc_srcs_pend = []
            inline_ga = [0, 0]

            def den_A(hv):
                hp, hh = hv // 2, hv % 2
                e1_, qb = rel_q[hp % 2]
                r3 = (blk * (H // 2) + hp) % 3
                e2_, skvb = rel_kv[r3]
                e3_, skb2 = rel_k[r3]
                rb = hp % 2
                s3 = hv % 3
                par = hv % 4
                if par == 0:
                    dpow = qe_a[:, :]
                elif par == 1:
                    dpow = qe_b[:, :]
                else:
                    dpow = fp(C_DPW + (par - 2) * 512, 512)
                gd_dpow[par].wait(s, ctx)
                s.wait_ge(S["sV"], ev_lndb)
                ev_dp = inc(s.activation(dpow, tp1_t[:, :], AF.Exp,
                                         scale=lnd_col[:, hv:hv + 1],
                                         bias=lndb_col[:, hv:hv + 1]), "sA")
                stv = fp(C_ST + s3 * 512, 512)
                gd_st[s3].wait(v, ctx)
                v.wait_ge(S["sA"], ev_dp)
                if hh == 0:
                    v.wait_ge(S["rkv%d" % r3], e2_[0])
                else:
                    v.wait_ge(S["rvh%d" % r3], e2_[1])
                v.wait_ge(S["sV"], ev_carryP)
                ev_st1 = inc(v.scalar_tensor_tensor(
                    out=stv, in0=dpow, scalar=carryP[:, hv:hv + 1],
                    in1=skvb[:, hh, :], op0=ALU.mult, op1=ALU.add), "sV")
                stk = fp(C_STK + s3 * 512, 512)
                gd_stk[s3].wait(v, ctx)
                if hh == 0:
                    v.wait_ge(S["rk%d" % r3], e3_[0])
                else:
                    v.wait_ge(S["rkh%d" % r3], e3_[1])
                ev_st2 = inc(v.scalar_tensor_tensor(
                    out=stk, in0=dpow, scalar=carryP[:, H + hv:H + hv + 1],
                    in1=skb2[:, hh, :], op0=ALU.mult, op1=ALU.add), "sV")
                gd_dpow[par].set(("sV", ev_st1), ("sV", ev_st2))
                qsb = sp(C_QS + par * 512, 512)
                gd_qs[par].wait(g, ctx)
                g.wait_ge(S["sV"], ev_st2)
                g.wait_ge(S["rq%d" % rb], e1_)
                ev_qs = inc(g.tensor_mul(qsb, qb[:, hh, :], stk), "sP")
                gd_stk[s3].set(("sP", ev_qs))
                dst_[hv] = dict(ev_st1=ev_st1, ev_st2=ev_st2, ev_qs=ev_qs,
                                qb=qb, hh=hh, stv=stv, qsb=qsb)

            def den_B(hv):
                par = hv % 4
                par2 = hv % 2
                d = dst_[hv]
                pd = psD[par2]
                gd_psD[par2].wait(t, ctx)
                t.wait_ge(S["sP"], d["ev_qs"])
                ev_den = inc(t.matmul(pd[0:1, :], onescolbf_t[:, :], d["qsb"],
                                      start=True, stop=True), "sT")
                gd_qs[par].set(("sT", ev_den))
                gd_rows[par2].wait(s, ctx)
                s.wait_ge(S["sT"], ev_den)
                ev_ln = inc(s.activation(lnrows[0:1, par2, :], pd[0:1, :], AF.Ln), "sA")
                s.wait_ge(S["sA"], ev_ln)
                ev_ex = inc(s.activation(denrows[0:1, par2, :], lnrows[0:1, par2, :],
                                         AF.Exp, scale=-1.0), "sA")
                gd_psD[par2].set(("sA", ev_ln))
                rbf = rminbf[0:1, par2, 0:BLK]
                v.wait_ge(S["sA"], ev_ex)
                if prev_bc[par2]:
                    v.wait_ge(S["sT"], prev_bc[par2])
                ev_mn = inc(v.tensor_scalar_min(rbf, denrows[0:1, par2, :], 1e6), "sV")
                d["ev_mn"] = ev_mn
                d["rbf"] = rbf

            def den_C(hv):
                par = hv % 4
                par2 = hv % 2
                d = dst_[hv]
                pb = psA[par]
                gd_psAc[par].wait(t, ctx)
                t.wait_ge(S["sV"], d["ev_mn"])
                ev_bc = inc(t.matmul(pb[:, :], onesrowbf_t[:, :], d["rbf"],
                                     start=True, stop=True), "sT")
                prev_bc[par2] = ev_bc
                gd_rows[par2].set(("sT", ev_bc))
                t2 = fp(C_T2 + par2 * 512, 512)
                gd_t2[par2].wait(v, ctx)
                v.wait_ge(S["sV"], d["ev_st1"])
                ev_t2 = inc(v.tensor_mul(t2, d["qb"][:, d["hh"], :], d["stv"]), "sV")
                v.wait_ge(S["sV"], ev_t2)
                v.wait_ge(S["sT"], ev_bc)
                ev_op = inc(v.tensor_mul(outpre[:, hv, :], t2, pb[:, :]), "sV")
                gd_t2[par2].set(("sV", ev_op))
                gd_psAc[par].set(("sV", ev_op))
                gd_st[hv % 3].set(("sV", ev_t2))
                ev_qs_ = d["ev_qs"]
                ev_st1_ = d["ev_st1"]
                ev_st2_ = d["ev_st2"]
                del dst_[hv]
                return ev_op, ev_qs_, ev_st1_, ev_st2_

            for hv in range(H + 2):
                if hv < 4:
                    issue_xre(hv)
                if hv < len(gacc_q):
                    ev9_, slot_ = gacc_q[hv]
                    g.wait_ge(S["sV"], ev9_)
                    g.wait_ge(S["sP"], prev_ga if prev_ga else ev_gz)
                    ev_ga = inc(g.tensor_add(gacc[:, :], gacc[:, :],
                                             gacc_srcs[hv]), "sP")
                    prev_ga = ev_ga
                    ln2_pool_g[slot_] = ev_ga
                if hv < H:
                    den_A(hv)
                if hv >= 1 and hv - 1 < H:
                    den_B(hv - 1)
                if hv >= 2:
                    ho = hv - 2
                    ev_op, ev_qs_, ev_st1_, ev_st2_ = den_C(ho)
                    ev_op_all = max(ev_op_all, ev_op)
                    if ho % 2 == 1:
                        pp = ho // 2
                        gd_qh[pp % 2].set(("sV", ev_op), ("sP", ev_qs_))
                        gp3 = (blk * (H // 2) + pp) % 3
                        gd_skv[gp3].set(("sV", ev_st1_))
                        gd_sk[gp3].set(("sV", ev_st2_))
                        if pp + 2 < H // 2:
                            issue_q(pp + 2, blk)
                        elif blk + 1 < NB:
                            issue_q(pp + 2 - H // 2, blk + 1)
                        if pp + 3 < H // 2:
                            issue_kv(pp + 3, blk)
                        elif blk + 1 < NB:
                            issue_kv(pp + 3 - H // 2, blk + 1)

            ev_x1sp_blk = []
            for j in range(4):
                x1j = fp(C_X1 + j * D, D)
                slot = j % 2
                mmev = []
                for ocb in range(4):
                    gd_psAc[ocb].wait(t, ctx)
                    if j == 0 and ocb == 0:
                        t.wait_ge(S["wout"], ev_wout)
                        t.wait_ge(S["sV"], ev_op_all)
                    for kc in range(NKC):
                        ins = t.matmul(psA[ocb][:, :],
                                       outpre[:, kc, j * 128:(j + 1) * 128],
                                       woutb2[:, kc, ocb * 512:(ocb + 1) * 512],
                                       start=(kc == 0), stop=(kc == NKC - 1))
                    mmev.append(inc(ins, "sT"))
                st0 = slot * 8
                sumsq = st_stats[:, st0:st0 + 1]
                sumr = st_stats[:, st0 + 1:st0 + 2]
                mean = st_stats[:, st0 + 2:st0 + 3]
                ex2 = st_stats[:, st0 + 3:st0 + 4]
                msq = st_stats[:, st0 + 4:st0 + 5]
                var = st_stats[:, st0 + 5:st0 + 6]
                rstd = st_stats[:, st0 + 6:st0 + 7]
                adds = []
                for ocb in range(4):
                    v.wait_ge(S["sT"], mmev[ocb])
                    v.wait_ge(S["xc%d" % j], ev_xre[j])
                    if stats_g[slot] and ocb == 0:
                        v.wait_ge(S["sV"], stats_g[slot])
                    ins = v.scalar_tensor_tensor(
                        out=x1j[:, ocb * 512:(ocb + 1) * 512],
                        in0=psA[ocb][:, :], scalar=1.0,
                        in1=x1j[:, ocb * 512:(ocb + 1) * 512],
                        op0=ALU.mult, op1=ALU.add,
                        accum_out=st_partials[:, 4 * slot + ocb:4 * slot + ocb + 1])
                    adds.append(inc(ins, "sV"))
                    gd_psAc[ocb].set(("sV", adds[-1]))
                ev_add = adds[-1]
                v.wait_ge(S["sV"], ev_add)
                ev_s1 = inc(v.reduce_sum(sumr, st_partials[:, 4 * slot:4 * slot + 4],
                                         axis=AX.X), "sV")
                gd_scrC.wait(s, ctx)
                s.wait_ge(S["sV"], ev_add)
                ev_sq = inc(s.activation(fp(C_SCR, D), x1j, AF.Square,
                                         accum_out=sumsq), "sA")
                gd_scrC.set(("sA", ev_sq))
                v.wait_ge(S["sA"], ev_sq)
                v.wait_ge(S["sV"], ev_s1)
                inc(v.tensor_scalar_mul(mean, sumr, 1.0 / D), "sV")
                ev3 = inc(v.tensor_scalar_mul(ex2, sumsq, 1.0 / D), "sV")
                v.wait_ge(S["sV"], ev3)
                ev4 = inc(v.tensor_scalar(msq, mean, mean, None, ALU.mult), "sV")
                v.wait_ge(S["sV"], ev4)
                ev5 = inc(v.tensor_sub(var, ex2, msq), "sV")
                s.wait_ge(S["sV"], ev5)
                ev6 = inc(s.activation(msq, var, AF.Ln, bias=EPS_LN), "sA")
                s.wait_ge(S["sA"], ev6)
                ev7 = inc(s.activation(rstd, msq, AF.Exp, scale=-0.5), "sA")
                ln2 = sp(C_LN2 + slot * D, D)
                gd_ln2[slot].wait(v, ctx)
                if j >= 2 and inline_ga[slot]:
                    v.wait_ge(S["sP"], inline_ga[slot])
                elif j < 2 and ln2_pool_g[slot]:
                    v.wait_ge(S["sP"], ln2_pool_g[slot])
                v.wait_ge(S["sA"], ev7)
                ev9 = inc(v.tensor_scalar(ln2, x1j, mean, rstd,
                                          ALU.subtract, ALU.mult), "sV")
                stats_g[slot] = ev9
                if j < 2:
                    g.wait_ge(S["sV"], ev9)
                    g.wait_ge(S["sP"], prev_ga if prev_ga else ev_gz)
                    ev_ga = inc(g.tensor_add(gacc[:, :], gacc[:, :], ln2), "sP")
                    prev_ga = ev_ga
                    inline_ga[slot] = ev_ga
                else:
                    gacc_pend.append((ev9, slot))
                    gacc_srcs_pend.append(ln2)
                q.wait_ge(S["sV"], ev9)
                q.wait_ge(S["sA"], ev_sq)
                ev_x1sp = dq("x1sp%d" % j,
                             x1_sp[blk * BLK + j * 128: blk * BLK + (j + 1) * 128, :], x1j)
                gd_x1[j].set(("x1sp%d" % j, ev_x1sp))
                evt = None
                sl2 = j % 2
                gd_ln2T[sl2].wait(t, ctx)
                for half in range(2):
                    gd_psT[half].wait(t, ctx)
                    t.wait_ge(S["sV"], ev9)
                    for cc in range(8):
                        c = half * 8 + cc
                        ins = t.transpose(psT[half][:, cc * 128:(cc + 1) * 128],
                                          ln2[:, c * 128:(c + 1) * 128], idbf_t[:, :])
                    evt = inc(ins, "sT")
                    v.wait_ge(S["sT"], evt)
                    ins = v.tensor_copy(
                        sp(C_LN2T + sl2 * NKC * 128, NKC * 128)
                        .rearrange("p (c m) -> p c m", c=NKC)[:, half * 8:(half + 1) * 8, :],
                        psT[half][:, :].rearrange("p (c m) -> p c m", c=8))
                    evv = inc(ins, "sV")
                    gd_psT[half].set(("sV", evv))
                gd_ln2[slot].set(("sT", evt))
                q.wait_ge(S["sV"], evv)
                ev_l2sp = dq("l2sp%d" % sl2,
                             ln2T_sp[:, blk * BLK + j * 128: blk * BLK + (j + 1) * 128]
                             .rearrange("(c p) t -> p c t", p=128),
                             sp(C_LN2T + sl2 * NKC * 128, NKC * 128)
                             .rearrange("p (c m) -> p c m", c=NKC))
                gd_ln2T[sl2].set(("l2sp%d" % sl2, ev_l2sp))
            gd_outpre.set(("sT", max(mmev)))

        for i_, (ev9_, _s) in enumerate(gacc_pend):
            g.wait_ge(S["sV"], ev9_)
            g.wait_ge(S["sP"], prev_ga if prev_ga else ev_gz)
            prev_ga = inc(g.tensor_add(gacc[:, :], gacc[:, :],
                                       gacc_srcs_pend[i_]), "sP")

        ctx.barrier()
        for c4 in range(4):
            par = c4 % 2
            gd_psD[par].wait(t, ctx)
            ev_gr = inc(t.matmul(psD[par][0:1, :], onescolf_t[:, :],
                                 gacc[:, c4 * 512:(c4 + 1) * 512],
                                 start=True, stop=True), "sT")
            v.wait_ge(S["sT"], ev_gr)
            ev_gc = inc(v.tensor_copy(grow[0:1, c4 * 512:(c4 + 1) * 512],
                                      psD[par][0:1, :]), "sV")
            gd_psD[par].set(("sV", ev_gc))
        g.wait_ge(S["sV"], ev_gc)
        ev_gin = dg("gin", g_in[:, :], grow[:, :])
        g.wait_ge(S["gin"], ev_gin)
        ev_cc2 = inc(g.collective_compute(
            "AllGather", ALU.bypass, replica_groups=[list(range(NCORES))],
            ins=[g_in.opt()], outs=[g_out.opt()]), "cc2")
        q.wait_ge(S["cc2"], ev_cc2)
        ev_go = dq("gout", gout_sb[:, :], g_out[:, :])
        for c4 in range(4):
            par = c4 % 2
            gd_psD[par].wait(t, ctx)
            t.wait_ge(S["gout"], ev_go)
            ev_ps = inc(t.matmul(psD[par][0:1, :], gpm_t[:, :],
                                 gout_sb[:, c4 * 512:(c4 + 1) * 512],
                                 start=True, stop=True), "sT")
            v.wait_ge(S["sT"], ev_ps)
            v.wait_ge(S["gin"], ev_gin)
            ev_mr = inc(v.tensor_scalar_mul(grow[0:1, c4 * 512:(c4 + 1) * 512],
                                            psD[par][0:1, :], 1.0 / TOT_SEQ), "sV")
            gd_psD[par].set(("sV", ev_mr))
        ev_gw = dq("cst", gWsb[:, :], gateW[:, :])
        gd_psAc[0].wait(t, ctx)
        t.wait_ge(S["sV"], ev_mr)
        for kc in range(NKC):
            ins = t.transpose(psA[0][:, kc:kc + 1],
                              grow[0:1, kc * 128:(kc + 1) * 128], idf_t[0:1, 0:1])
        ev_mt = inc(ins, "sT")
        v.wait_ge(S["sT"], ev_mt)
        ev_mtc = inc(v.tensor_copy(meanT[:, :], psA[0][:, 0:NKC]), "sV")
        gd_psAc[0].set(("sV", ev_mtc))
        gd_psAc[1].wait(t, ctx)
        t.wait_ge(S["cst"], ev_gw)
        for kc in range(NKC):
            ins = t.transpose(psA[1][:, kc * NE:(kc + 1) * NE],
                              gWsb[:, kc * 128:(kc + 1) * 128], idf_t[0:NE, 0:NE])
        ev_gt = inc(ins, "sT")
        v.wait_ge(S["sT"], ev_gt)
        ev_gtc = inc(v.tensor_copy(gWT[:, :], psA[1][:, 0:NKC * NE]), "sV")
        gd_psAc[1].set(("sV", ev_gtc))
        gd_psD[0].wait(t, ctx)
        t.wait_ge(S["sV"], ev_mtc)
        t.wait_ge(S["sV"], ev_gtc)
        for kc in range(NKC):
            ins = t.matmul(psD[0][0:1, 0:NE], meanT[:, kc:kc + 1],
                           gWT[:, kc * NE:(kc + 1) * NE],
                           start=(kc == 0), stop=(kc == NKC - 1))
        ev_lg = inc(ins, "sT")
        v.wait_ge(S["sT"], ev_lg)
        ev_lgc = inc(v.tensor_copy(lgrow[:, :], psD[0][0:1, 0:NE]), "sV")
        gd_psD[0].set(("sV", ev_lgc))
        gd_psD[1].wait(t, ctx)
        t.wait_ge(S["sV"], ev_lgc)
        ev_bc6 = inc(t.matmul(psD[1][:, 0:NE], onesrowf_t[:, :], lgrow[:, :],
                              start=True, stop=True), "sT")
        v.wait_ge(S["sT"], ev_bc6)
        ev = inc(v.tensor_copy(gG[:, :], psD[1][:, 0:NE]), "sV")
        gd_psD[1].set(("sV", ev))
        v.wait_ge(S["sV"], ev)
        ev = inc(v.reduce_max(gm1[:, 0:1], gG[:, :], axis=AX.X), "sV")
        v.wait_ge(S["sV"], ev)
        ev = inc(v.tensor_scalar(gneg[:, 0:1], gm1[:, 0:1], -1.0, None, ALU.mult), "sV")
        s.wait_ge(S["sV"], ev)
        ev = inc(s.activation(gE[:, :], gG[:, :], AF.Exp, bias=gneg[:, 0:1]), "sA")
        v.wait_ge(S["sA"], ev)
        ev = inc(v.reduce_sum(gsum[:, 0:1], gE[:, :], axis=AX.X), "sV")
        v.wait_ge(S["sV"], ev)
        ev = inc(v.reciprocal(grv[:, 0:1], gsum[:, 0:1]), "sV")
        v.wait_ge(S["sV"], ev)
        ev = inc(v.tensor_scalar_mul(gP[:, :], gE[:, :], grv[:, 0:1]), "sV")
        v.wait_ge(S["sV"], ev)
        ev = inc(v.reduce_max(gm1[:, 0:1], gP[:, :], axis=AX.X), "sV")
        v.wait_ge(S["sV"], ev)
        ev = inc(v.tensor_scalar(gmask1[:, :], gP[:, :], gm1[:, 0:1], None,
                                 ALU.is_equal), "sV")
        v.wait_ge(S["sV"], ev)
        ev = inc(v.tensor_scalar_mul(gtmp[:, :], gmask1[:, :], gm1[:, 0:1]), "sV")
        v.wait_ge(S["sV"], ev)
        ev = inc(v.tensor_sub(gp2[:, :], gP[:, :], gtmp[:, :]), "sV")
        v.wait_ge(S["sV"], ev)
        ev = inc(v.reduce_max(gm2[:, 0:1], gp2[:, :], axis=AX.X), "sV")
        v.wait_ge(S["sV"], ev)
        ev = inc(v.tensor_scalar(gmask2[:, :], gp2[:, :], gm2[:, 0:1], None,
                                 ALU.is_equal), "sV")
        v.wait_ge(S["sV"], ev)
        ev = inc(v.tensor_add(gvs[:, 0:1], gm1[:, 0:1], gm2[:, 0:1]), "sV")
        v.wait_ge(S["sV"], ev)
        ev = inc(v.reciprocal(grv[:, 0:1], gvs[:, 0:1]), "sV")
        v.wait_ge(S["sV"], ev)
        ev = inc(v.tensor_mul(gw1[:, 0:1], gm1[:, 0:1], grv[:, 0:1]), "sV")
        v.wait_ge(S["sV"], ev)
        ev_w2c = inc(v.tensor_mul(gw2[:, 0:1], gm2[:, 0:1], grv[:, 0:1]), "sV")
        v.wait_ge(S["sV"], ev_w2c)
        ev = inc(v.scalar_tensor_tensor(out=gtmp[:, :], in0=gmask1[:, :], scalar=1.0,
                                        in1=iota6_t[:, :], op0=ALU.mult, op1=ALU.mult,
                                        accum_out=geidx[0][:, 0:1]), "sV")
        v.wait_ge(S["sV"], ev)
        ev_idx = inc(v.scalar_tensor_tensor(out=gtmp[:, :], in0=gmask2[:, :], scalar=1.0,
                                            in1=iota6_t[:, :], op0=ALU.mult, op1=ALU.mult,
                                            accum_out=geidx[1][:, 0:1]), "sV")
        for sl in range(2):
            v.wait_ge(S["sV"], ev_idx)
            ev = inc(v.tensor_scalar_mul(gesc[sl][:, 0:1], geidx[sl][:, 0:1],
                                         float(D)), "sV")
            v.wait_ge(S["sV"], ev)
            ev = inc(v.tensor_scalar(idxf[:, :], iotaW1_t[:, :], gesc[sl][:, 0:1],
                                     None, ALU.add), "sV")
            v.wait_ge(S["sV"], ev)
            ev = inc(v.tensor_copy(idxi1[sl][:, :], idxf[:, :]), "sV")
            v.wait_ge(S["sV"], ev)
            ev = inc(v.tensor_scalar_mul(gesc[sl][:, 0:1], geidx[sl][:, 0:1],
                                         float(FF)), "sV")
            v.wait_ge(S["sV"], ev)
            ev = inc(v.tensor_scalar(idxf2[:, :], iotaW2_t[:, :], gesc[sl][:, 0:1],
                                     None, ALU.add), "sV")
            v.wait_ge(S["sV"], ev)
            ev = inc(v.tensor_copy(idxi2[sl][:, :], idxf2[:, :]), "sV")
            v.wait_ge(S["sV"], ev)
            ev = inc(v.tensor_copy(idxbi[:, sl:sl + 1], geidx[sl][:, 0:1]), "sV")
        ev_idx_done = ev
        ev_b1c = 0
        for sl in range(2):
            g.wait_ge(S["sV"], ev_idx_done)
            if sl > 0:
                g.wait_ge(S["sV"], ev_b1c)
            ev_b1 = inc(g.indirect_dma_start(
                out=wgather[:, 0:FF], out_offset=None, in_=b1d[:, :],
                in_offset=IndirectOffsetOnAxis(idxbi[:, sl:sl + 1], 0)), "bg", 16)
            v.wait_ge(S["bg"], ev_b1)
            ev_b1c = inc(v.tensor_copy(b1row[0:1, sl, :], wgather[0:1, 0:FF]), "sV")
        g.wait_ge(S["sV"], ev_b1c)
        ev_b2 = inc(g.indirect_dma_start(
            out=wgather[:, 0:D], out_offset=None, in_=b2d[:, :],
            in_offset=IndirectOffsetOnAxis(idxbi[:, 0:1], 0)), "bg", 16)
        v.wait_ge(S["bg"], ev_b2)
        ev = inc(v.tensor_scalar_mul(grow[0:1, :], wgather[0:1, 0:D], gw1[0:1, 0:1]), "sV")
        g.wait_ge(S["sV"], ev)
        ev_b2b = inc(g.indirect_dma_start(
            out=wgather[:, 0:D], out_offset=None, in_=b2d[:, :],
            in_offset=IndirectOffsetOnAxis(idxbi[:, 1:2], 0)), "bg", 16)
        v.wait_ge(S["bg"], ev_b2b)
        v.wait_ge(S["sV"], ev)
        ev = inc(v.scalar_tensor_tensor(out=gacc[0:1, 0:D], in0=wgather[0:1, 0:D],
                                        scalar=gw2[0:1, 0:1], in1=grow[0:1, :],
                                        op0=ALU.mult, op1=ALU.add), "sV")
        v.wait_ge(S["sV"], ev)
        ev_b2c = inc(v.tensor_copy(b2crow[0:1, :], gacc[0:1, 0:D]), "sV")

        ctx.barrier()
        W1s = bigbf[:, 0:NKC * FF].rearrange("p (c f) -> p c f", c=NKC)
        W1flat = W1d.rearrange("e c f -> (e c) f")
        W2flat = W2d.rearrange("e f c -> (e f) c")
        wstage = fpool[:, 8576:8576 + 704].bitcast(BF16)
        wbg = [fpool[:, 14336 + i_ * FF:14336 + (i_ + 1) * FF] for i_ in range(4)]
        wbg2 = [fpool[:, 19968 + i_ * 640:19968 + (i_ + 1) * 640] for i_ in range(2)]
        gd_bg2 = [Gd(), Gd()]
        gd_wgs = [Gd(), Gd()]
        gd_bg = [Gd(), Gd(), Gd(), Gd()]
        gd_stage = Gd()
        gd_hst = Gd()
        gd_ln2Te = Gd()
        gd_psAe = [Gd() for _ in range(4)]
        for b_ in range(4):
            gd_psAe[b_].set(("sV", ctx.C["sV"]), ("sA", ctx.C["sA"]))

        def bg_gather(ti):
            wg = wbg[ti % 4]
            gd_bg[ti % 4].wait(g, ctx)
            g.wait_ge(S["sV"], ev_idx_done)
            if ti < NKC:
                sl, c = 1, ti
                ev = inc(g.indirect_dma_start(
                    out=wg[:, 0:FF], out_offset=None, in_=W1flat,
                    in_offset=IndirectOffsetOnAxis(idxi1[sl][:, c:c + 1], 0)),
                    "bgg%d" % (ti % 4), 16)
            else:
                t2i = ti - NKC
                sl, c = t2i // NFFC, t2i % NFFC
                gd_bg2[ti % 2].wait(g, ctx)
                ev = inc(g.indirect_dma_start(
                    out=wg[:, 0:1408], out_offset=None, in_=W2flat,
                    in_offset=IndirectOffsetOnAxis(idxi2[sl][:, c:c + 1], 0)),
                    "bgg%d" % (ti % 4), 16)
                ev = inc(g.indirect_dma_start(
                    out=fpool[:, 19968 + (ti % 4) * 384:19968 + (ti % 4) * 384 + 384]
                    if False else wbg2[ti % 2], out_offset=None, in_=W2flat,
                    in_offset=IndirectOffsetOnAxis(idxi2[sl][:, c:c + 1], 0),
                    element_offset=1408), "bgg%d" % (ti % 4), 16)
            bg_gev[ti] = ev

        def bg_convert(ti):
            if ti < NKC:
                sl, c = 1, ti
                wg = wbg[ti % 4]
                v.wait_ge(S["bgg%d" % (ti % 4)], bg_gev[ti])
                gd_stage.wait(v, ctx)
                ev_c1 = inc(v.tensor_copy(wstage[:, 0:FF], wg[:, 0:FF]), "sV")
                gd_bg[ti % 4].set(("sV", ev_c1))
                q.wait_ge(S["sV"], ev_c1)
                ev_st = dq("wstg", w1s_bf[c * 128:(c + 1) * 128, :], wstage[:, 0:FF])
                gd_stage.set(("wstg", ev_st))
            else:
                t2i = ti - NKC
                sl, c = t2i // NFFC, t2i % NFFC
                wg = wbg[ti % 4]
                v.wait_ge(S["bgg%d" % (ti % 4)], bg_gev[ti])
                gd_stage.wait(v, ctx)
                ev_c1 = inc(v.tensor_scalar_mul(wstage[:, 0:1408], wg[:, 0:1408],
                                                (gw1 if sl == 0 else gw2)[:, 0:1]), "sV")
                q.wait_ge(S["sV"], ev_c1)
                ev_s1 = dq("wstg", w2s_bf[sl, c * 128:(c + 1) * 128, 0:1408],
                           wstage[:, 0:1408])
                v.wait_ge(S["wstg"], ev_s1)
                ev_c2 = inc(v.tensor_scalar_mul(wstage[:, 0:640], wbg2[ti % 2][:, 0:640],
                                                (gw1 if sl == 0 else gw2)[:, 0:1]), "sV")
                gd_bg[ti % 4].set(("sV", ev_c2))
                gd_bg2[ti % 2].set(("sV", ev_c2))
                q.wait_ge(S["sV"], ev_c2)
                ev_st = dq("wstg", w2s_bf[sl, c * 128:(c + 1) * 128, 1408:2048],
                           wstage[:, 0:640])
                gd_stage.set(("wstg", ev_st))
            bg_done[ti] = ev_st

        NT_BG = NKC + 2 * NFFC
        bg_done = [0] * NT_BG
        bg_gev = [0] * NT_BG
        bg_gi = 0
        bg_i = 0

        w1evs = []
        for c in range(NKC):
            ws = c % 2
            wg = wgather[:, 0:FF] if ws == 0 else fpool[:, E_OUT:E_OUT + FF]
            gd_wgs[ws].wait(g, ctx)
            g.wait_ge(S["sV"], ev_idx_done)
            ev_ga_ = inc(g.indirect_dma_start(
                out=wg, out_offset=None, in_=W1flat,
                in_offset=IndirectOffsetOnAxis(idxi1[0][:, c:c + 1], 0)), "w1g%d" % ws, 16)
            eng = v if ws == 0 else s
            ekey = "sV" if ws == 0 else "sA"
            eng.wait_ge(S["w1g%d" % ws], ev_ga_)
            if ws == 0:
                ev_cv = inc(v.tensor_copy(W1s[:, c, :], wg), "sV")
            else:
                ev_cv = inc(s.activation(W1s[:, c, :], wg, AF.Copy), "sA")
            gd_wgs[ws].set((ekey, ev_cv))
            w1evs.append((ekey, ev_cv))

        hstg2 = [sp(E_HST + i_ * 1024, 1024).rearrange("p (c t) -> p c t", c=2)
                 for i_ in range(2)]
        ln2Tb2 = [sp(E_LN2T + i_ * NKC * 512, NKC * 512)
                  .rearrange("p (c t) -> p c t", c=NKC) for i_ in range(2)]
        gd_l2r = [Gd(), Gd()]
        gd_hst2 = [Gd(), Gd()]
        for sl in range(2):
            if sl == 1:
                while bg_i < NKC:
                    if bg_gi < NKC and bg_gi < bg_i + 4:
                        bg_gather(bg_gi)
                        bg_gi += 1
                    else:
                        bg_convert(bg_i)
                        bg_i += 1
                q.wait_ge(S["wstg"], max(bg_done[0:NKC]))
                t.wait_ge(S["sT"], w1_rd_ev)
                q.wait_ge(S["sT"], w1_rd_ev)
                ev_w1l = dq("wld", W1s[:, :, :],
                            w1s_bf.rearrange("(c p) f -> p c f", p=128))
                t.wait_ge(S["wld"], ev_w1l)
            for blk in range(NB):
                bslot = blk % 2
                ln2Tb = ln2Tb2[bslot]
                gd_l2r[bslot].wait(s, ctx)
                ev_l2 = dsc("l2r%d" % bslot, ln2Tb[:, :, :],
                            ln2T_sp[:, blk * BLK:(blk + 1) * BLK]
                            .rearrange("(c p) t -> p c t", p=128))
                for ffc in range(NFFC):
                    bank = ffc % 2
                    hs = (ffc // 2) % 2
                    gd_psAe[bank].wait(t, ctx)
                    t.wait_ge(S["l2r%d" % bslot], ev_l2)
                    t.wait_ge(S["sV"], ev_b1c)
                    for kc in range(NKC):
                        if sl == 0 and blk == 0 and ffc == 0:
                            ek_, ev_ = w1evs[kc]
                            t.wait_ge(S[ek_], ev_)
                        t.matmul(psA[bank][:, :], W1s[:, kc, ffc * 128:(ffc + 1) * 128],
                                 ln2Tb[:, kc, :], start=(kc == 0), stop=False)
                    ins = t.matmul(psA[bank][:, :], b1row[0:1, sl, ffc * 128:(ffc + 1) * 128],
                                   ones512bf_t[:, :], start=False, stop=True)
                    ev_mm = inc(ins, "sT")
                    s.wait_ge(S["sT"], ev_mm)
                    if ffc % 2 == 0:
                        gd_hst2[hs].wait(s, ctx)
                    ev_h = inc(s.activation(hstg2[hs][:, ffc % 2, :],
                                            psA[bank][:, :], AF.Gelu), "sA")
                    gd_psAe[bank].set(("sA", ev_h))
                    if ffc % 2 == 1 or ffc == NFFC - 1:
                        nff = 2 if ffc % 2 == 1 else 1
                        q.wait_ge(S["sA"], ev_h)
                        ev_hsp = dq("hsp%d" % hs,
                                    h_sp[sl, (ffc - nff + 1) * 128:(ffc + 1) * 128,
                                         blk * BLK:(blk + 1) * BLK]
                                    .rearrange("(c p) t -> p c t", p=128),
                                    hstg2[hs][:, 0:nff, :])
                        gd_hst2[hs].set(("hsp%d" % hs, ev_hsp))
                gd_l2r[bslot].set(("sT", ev_mm))
                nbg = 3 if sl == 0 else 4
                for _ in range(nbg):
                    lead = 4 if bg_gi < NKC else 2
                    if bg_gi < NT_BG and bg_gi < bg_i + lead:
                        bg_gather(bg_gi)
                        bg_gi += 1
                for _ in range(nbg):
                    if bg_i < bg_gi:
                        bg_convert(bg_i)
                        bg_i += 1
            w1_rd_ev = ctx.C["sT"]
        while bg_i < NT_BG:
            lead = 4 if bg_gi < NKC else 2
            if bg_gi < NT_BG and bg_gi < bg_i + lead:
                bg_gather(bg_gi)
                bg_gi += 1
            else:
                bg_convert(bg_i)
                bg_i += 1
        ev_w2stg = ctx.C["wstg"]

        ctx.barrier()
        W2s = bigbf[:, 0:2 * NFFC * 1024].rearrange("p (s c m) -> p s c m", s=2, c=NFFC)
        gd_hb = [Gd(), Gd()]
        gd_x1e = [Gd(), Gd()]
        gd_oute = [Gd(), Gd()]
        gd_psAe2 = [Gd() for _ in range(4)]
        for b_ in range(4):
            gd_psAe2[b_].set(("sA", ctx.C["sA"]), ("sV", ctx.C["sV"]))
        w2_rd_prev = 0
        for ochalf in range(2):
            if ochalf > 0:
                q.wait_ge(S["sT"], w2_rd_prev)
            w2l = []
            q.wait_ge(S["wstg"], ev_w2stg)
            for sl in range(2):
                ev_ = dq("wld", W2s[:, sl, :, :],
                         w2s_bf[sl].rearrange("(c p) m -> p c m", p=128)
                         [:, :, ochalf * 1024:(ochalf + 1) * 1024])
                w2l.append(ev_)
            ev_w2 = max(w2l)
            w2_rd_ev = 0
            for blk in range(NB):
                for jt in range(4):
                    pp = jt % 2
                    x1b = fp(E_X1 + pp * D, D)
                    gd_x1e[pp].wait(q, ctx)
                    ev_x1 = dq("x1r%d" % pp, x1b,
                               x1_sp[blk * BLK + jt * 128: blk * BLK + (jt + 1) * 128, :])
                    hb = sp(E_HBF + pp * 2 * NFFC * 128, 2 * NFFC * 128).rearrange(
                        "p (s c m) -> p s c m", s=2, c=NFFC)
                    gd_hb[pp].wait(s, ctx)
                    ev_h = dsc("hr%d" % pp, hb[:, :, :, :],
                               h_sp[:, :, blk * BLK + jt * 128: blk * BLK + (jt + 1) * 128]
                               .rearrange("s (c p) t -> p s c t", p=128))
                    outb = fp(E_OUT + pp * 1024, 1024)
                    oev = []
                    for oc2 in range(2):
                        gd_psAe2[2 + oc2].wait(t, ctx)
                    t.wait_ge(S["wld"], ev_w2)
                    t.wait_ge(S["hr%d" % pp], ev_h)
                    t.wait_ge(S["sV"], ev_b2c)
                    first = [True, True]
                    ins2 = [None, None]
                    for sl in range(2):
                        for ffc in range(NFFC):
                            for oc2 in range(2):
                                ins2[oc2] = t.matmul(
                                    psA[2 + oc2][:, :], hb[:, sl, ffc, :],
                                    W2s[:, sl, ffc, oc2 * 512:(oc2 + 1) * 512],
                                    start=first[oc2], stop=False)
                                first[oc2] = False
                    for oc2 in range(2):
                        ins2[oc2] = t.matmul(psA[2 + oc2][:, :], onesrowbf_t[:, :],
                                             b2crow[0:1, ochalf * 1024 + oc2 * 512:
                                                    ochalf * 1024 + (oc2 + 1) * 512],
                                             start=False, stop=True)
                        ev_mm = inc(ins2[oc2], "sT")
                        w2_rd_ev = max(w2_rd_ev, ev_mm)
                        v.wait_ge(S["sT"], ev_mm)
                        v.wait_ge(S["x1r%d" % pp], ev_x1)
                        if oc2 == 0:
                            gd_oute[pp].wait(v, ctx)
                        ins = v.tensor_add(outb[:, oc2 * 512:(oc2 + 1) * 512],
                                           psA[2 + oc2][:, :],
                                           x1b[:, ochalf * 1024 + oc2 * 512:
                                               ochalf * 1024 + (oc2 + 1) * 512])
                        ev_o = inc(ins, "sV")
                        gd_psAe2[2 + oc2].set(("sV", ev_o))
                        oev.append(ev_o)
                    gd_hb[pp].set(("sT", w2_rd_ev))
                    gd_x1e[pp].set(("sV", max(oev)))
                    q.wait_ge(S["sV"], max(oev))
                    ev_out = dq("out%d" % pp,
                                out_d[blk * BLK + jt * 128: blk * BLK + (jt + 1) * 128,
                                      ochalf * 1024:(ochalf + 1) * 1024], outb)
                    gd_oute[pp].set(("out%d" % pp, ev_out))
            w2_rd_prev = w2_rd_ev
        q.wait_ge(S["out0"], ctx.C["out0"])
        q.wait_ge(S["out1"], ctx.C["out1"])

    with nc.Block() as block:
        def mk(real_key):
            def body(eng):
                engines = {k: _Fake() for k in "qsvtg"}
                engines[real_key] = eng
                walk(Ctx(engines, sems))
            return body
        block.sync(mk("q"))
        block.scalar(mk("s"))
        block.vector(mk("v"))
        block.tensor(mk("t"))
        block.gpsimd(mk("g"))

    es.close()
    nc.finalize()
    return nc


_NC_CACHE = {}


def _get_nc(T):
    if T not in _NC_CACHE:
        _NC_CACHE[T] = build_nc(T)
    return _NC_CACHE[T]


def kernel(L0_attn_x, L0_attn_w_qkv, L0_attn_w_out, L0_attn_decay_param,
           L0_ffn_gate_W, L0_ffn_W1, L0_ffn_b1, L0_ffn_W2, L0_ffn_b2):
    x = np.ascontiguousarray(np.asarray(L0_attn_x, dtype=np.float32))
    B, TT, Dm = x.shape
    assert Dm == D and B * TT == NCORES * (TT // 2)
    T = TT // 2
    nc_ = _get_nc(T)

    wq = np.ascontiguousarray(np.asarray(L0_attn_w_qkv, dtype=np.float32))
    wo = np.ascontiguousarray(np.asarray(L0_attn_w_out, dtype=np.float32))
    dec = np.ascontiguousarray(np.asarray(L0_attn_decay_param, dtype=np.float32)).reshape(1, H)
    gw = np.ascontiguousarray(np.asarray(L0_ffn_gate_W, dtype=np.float32))
    w1 = np.ascontiguousarray(np.asarray(L0_ffn_W1, dtype=np.float32))
    bb1 = np.ascontiguousarray(np.asarray(L0_ffn_b1, dtype=np.float32))
    w2 = np.ascontiguousarray(np.asarray(L0_ffn_W2, dtype=np.float32))
    bb2 = np.ascontiguousarray(np.asarray(L0_ffn_b2, dtype=np.float32))

    in_maps = []
    for c in range(NCORES):
        b, half = c // 2, c % 2
        xs = np.ascontiguousarray(x[b, half * T:(half + 1) * T, :])
        pm = np.zeros((128, NCORES), np.float32)
        if half == 1:
            pm[:, c - 1] = 1.0
        gpm = np.zeros((NCORES, 1), np.float32)
        gpm[2 * b, 0] = 1.0
        gpm[2 * b + 1, 0] = 1.0
        in_maps.append({
            "x": xs, "wqkv": wq, "wout": wo, "decay": dec, "gateW": gw,
            "W1": w1, "b1": bb1, "W2": w2, "b2": bb2,
            "pmask": pm, "gpmask": gpm,
        })

    res = run_bass_kernel_spmd(nc_, in_maps, core_ids=list(range(NCORES)))
    outp = np.empty((B, TT, D), dtype=np.float32)
    for c in range(NCORES):
        b, half = c // 2, c % 2
        outp[b, half * T:(half + 1) * T, :] = res.results[c]["out"]
    return outp
